# version 14
# baseline (speedup 1.0000x reference)
import numpy as np
import ml_dtypes
from contextlib import ExitStack

import concourse.bass as bass
import concourse.mybir as mybir
from concourse.bass_utils import run_bass_kernel_spmd

F32 = mybir.dt.float32
F16 = mybir.dt.float16
BF16 = mybir.dt.bfloat16
U8 = mybir.dt.uint8
AF = mybir.ActivationFunctionType
ALU = mybir.AluOpType
AX = mybir.AxisListType

D = 2048
SEQ = 8192
NCORE = 8
TOK = SEQ // NCORE
NT = TOK // 128
HB = D // 128
NI = 64
TH = 512


def build_program():
    nc = bass.Bass("TRN2", target_bir_lowering=False)

    x_d = nc.dram_tensor("x", [TOK, D], F32, kind="ExternalInput")
    wgT_d = nc.dram_tensor("wgT", [D, 8192], U8, kind="ExternalInput")
    wuT_d = nc.dram_tensor("wuT", [D, 8192], U8, kind="ExternalInput")
    wdT_d = nc.dram_tensor("wdT", [8192, D], U8, kind="ExternalInput")
    gsc_d = nc.dram_tensor("gsc", [128, NI], F32, kind="ExternalInput")
    usc_d = nc.dram_tensor("usc", [128, NI], F32, kind="ExternalInput")
    dsc_d = nc.dram_tensor("dsc", [128, HB], F32, kind="ExternalInput")
    bias_d = nc.dram_tensor("biasr", [128, HB], F32, kind="ExternalInput")
    w0_d = nc.dram_tensor("w0r", [128, HB], F32, kind="ExternalInput")
    w0rev_d = nc.dram_tensor("w0rev", [128, HB], F32, kind="ExternalInput")
    w1_d = nc.dram_tensor("w1row", [1, D], F32, kind="ExternalInput")
    dv_d = nc.dram_tensor("dvrow", [1, 2], F32, kind="ExternalInput")
    eall_d = nc.dram_tensor("eall", [128, D], F16, kind="ExternalInput")
    mt_d = nc.dram_tensor("mtile", [128, 128], BF16, kind="ExternalInput")
    idb_d = nc.dram_tensor("idbf", [128, 128], BF16, kind="ExternalInput")
    idf_d = nc.dram_tensor("idf32", [128, 128], F32, kind="ExternalInput")
    lnscr_d = nc.dram_tensor("lnscr", [1, 2], F32, kind="Internal")
    y_d = nc.dram_tensor("y", [TOK, D], F32, kind="ExternalOutput")

    x3 = x_d.ap().rearrange("(j p) f -> p j f", p=128)
    y3 = y_d.ap().rearrange("(j p) f -> p j f", p=128)
    wgT3 = wgT_d.ap().rearrange("(hb p) i -> p hb i", p=128)
    wuT3 = wuT_d.ap().rearrange("(hb p) i -> p hb i", p=128)
    wdT3 = wdT_d.ap().rearrange("(ib p) h -> p ib h", p=128)

    ctx = ExitStack()

    def sb(name, shape, dt):
        return ctx.enter_context(nc.sbuf_tensor(name, shape, dt))

    xnT = sb("xnT", [128, HB * TOK], BF16)
    gout = sb("gout", [128, HB * TOK], F32)
    arena = sb("arena", [128, 65536], U8)
    wgu = sb("wgu", [128, 2, D], U8)
    wgb = sb("wgb", [128, 2, D], BF16)
    wuu = sb("wuu", [128, 2, D], U8)
    wub = sb("wub", [128, 2, D], BF16)
    wdu = sb("wdu", [128, 4, 1024], U8)
    wdb = sb("wdb", [128, 2, 1024], BF16)
    at_t = sb("at", [128, 2, TH], BF16)
    bt_t = sb("bt", [128, 2, TH], BF16)
    eall = sb("ealls", [128, D], F16)
    w1b = sb("w1b", [128, D], BF16)
    mt = sb("mt", [128, 128], BF16)
    idb = sb("idb", [128, 128], BF16)
    idf = sb("idf", [128, 128], F32)
    gsc = sb("gscs", [128, NI], F32)
    usc = sb("uscs", [128, NI], F32)
    dsc = sb("dscs", [128, HB], F32)
    biasr = sb("biasrs", [128, HB], F32)
    w0r = sb("w0rs", [128, HB], F32)
    w0rev = sb("w0revs", [128, HB], F32)
    dvs = sb("dvs", [1, 2], F32)
    dvc = sb("dvc", [1, 2], F32)
    lndv = sb("lndv", [1, 2], F32)
    lnrep = sb("lnrep", [128, 2], F32)
    stats = sb("stats", [128, 2, 8], F32)

    PS = ctx.enter_context(nc.psum_tensor("PS", [128, 8, 512], F32))

    A = arena.ap()

    def xbuf(t):
        b = t % 4
        return A[:, b * 8192:(b + 1) * 8192].bitcast(F32)

    def xnt_t(t):
        b = t % 4
        return A[:, 32768 + b * 4096:32768 + (b + 1) * 4096].bitcast(BF16)

    scr = A[:, 49152:57344].bitcast(F32)

    hp_all = A[:, 0:65536].bitcast(BF16)

    def hp(i):
        return hp_all[:, i * TH:(i + 1) * TH]

    def strip0(sbi):
        return A[:, sbi * 4096:(sbi + 1) * 4096].bitcast(BF16)

    def strip1(sbi):
        return A[:, 20480 + sbi * 4096:20480 + (sbi + 1) * 4096].bitcast(BF16)

    def pTv(pb):
        return A[:, 38912 + pb * 4096:38912 + (pb + 1) * 4096].bitcast(F32)

    def xslv(pb):
        return A[:, 47104 + pb * 4096:47104 + (pb + 1) * 4096].bitcast(F32)

    def obufv(pb):
        return A[:, 55296 + pb * 4096:55296 + (pb + 1) * 4096].bitcast(F32)

    psap = PS.ap()

    def psb(k):
        return psap[:, k, :]

    def psb_bf(k):
        return psap[:, k, :].bitcast(BF16)

    ENGS = ("sync", "scalar", "vector", "tensor", "gpsimd")
    steps = {e: [] for e in ENGS}
    semv = {s: 0 for s in ("dW", "dX", "sT", "sS", "sV", "sO")}

    def emit(eng, fn, waits=(), inc=None, amt=1):
        for (s, v) in waits:
            assert v is not None, "wait references an un-emitted milestone"
        ws = tuple((s, v) for (s, v) in waits if v > 0)
        steps[eng].append((ws, fn, inc, amt))
        if inc is not None:
            semv[inc] += amt
        return semv[inc] if inc is not None else 0

    bank_free = [None] * 8

    def bank_waits(banks):
        ws = []
        for k in banks:
            if bank_free[k] is not None and bank_free[k] not in ws:
                ws.append(bank_free[k])
        return ws

    cl = [(dvs, dv_d), (idb, idb_d), (gsc, gsc_d), (usc, usc_d), (dsc, dsc_d),
          (biasr, bias_d), (w0r, w0_d), (w0rev, w0rev_d), (idf, idf_d), (mt, mt_d), (eall, eall_d)]
    cl_done = {}
    for dst, src in cl:
        dW_consts = emit(
            "sync",
            (lambda dst, src: lambda e: e.dma_start(out=dst[:], in_=src.ap()))(dst, src),
            inc="dW", amt=16)
        cl_done[dst.name] = dW_consts

    v = emit("vector", lambda e: e.tensor_scalar(
        out=dvc[:], in0=dvs[:], scalar1=0.9, scalar2=None, op0=ALU.max),
        waits=[("dW", cl_done["dvs"])], inc="sV")
    v = emit("vector", lambda e: e.tensor_scalar(
        out=dvc[:], in0=dvc[:], scalar1=1.0, scalar2=None, op0=ALU.min),
        waits=[("sV", v)], inc="sV")
    emit("scalar", lambda e: e.preload_act_table(AF.Ln))
    sS_ln = emit("scalar", lambda e: e.activation(out=lndv[:], in_=dvc[:], func=AF.Ln),
                 waits=[("sV", v)], inc="sS")
    emit("scalar", lambda e: e.preload_act_table(AF.Sqrt))
    dw = emit("sync", lambda e: e.dma_start(out=lnscr_d.ap(), in_=lndv[:]),
              waits=[("sS", sS_ln)], inc="dW", amt=16)
    dW_lnrep = emit("sync", lambda e: e.dma_start(
        out=lnrep[:], in_=lnscr_d.ap().broadcast_to((128, 2))),
        waits=[("dW", dw)], inc="dW", amt=16)

    emit("gpsimd", lambda e: e.dma_start(
        out=w1b[:], in_=w1_d.ap().broadcast_to((128, D))), inc="sO", amt=16)

    NIT = 2 * NI
    wgld = [None] * NIT
    wuld = [None] * NIT
    deqg = [None] * NIT
    dequ = [None] * NIT
    mmg = [None] * NIT
    mmu = [None] * NIT
    evA = [None] * NIT
    evB = [None] * NIT
    hmul = [None] * NIT
    NDIT = 4 * NI
    wdld = [None] * NDIT
    deqd = [None] * NDIT
    mmd = [None] * NDIT
    evD = {}

    def p1_dma(it):
        i = it % NI
        ws = [deqg[it - 2]] if it >= 2 else []
        wgld[it] = emit("sync", (lambda it, i: lambda e: e.dma_start(
            out=wgu[:, it % 2, :], in_=wgT3[:, :, i * 128:(i + 1) * 128]))(it, i),
            waits=ws, inc="dW", amt=16)
        ws = [dequ[it - 2]] if it >= 2 else []
        wuld[it] = emit("sync", (lambda it, i: lambda e: e.dma_start(
            out=wuu[:, it % 2, :], in_=wuT3[:, :, i * 128:(i + 1) * 128]))(it, i),
            waits=ws, inc="dW", amt=16)

    def p1_deqg(it, eng="scalar"):
        ws = [("dW", wgld[it])]
        if it >= 2:
            ws.append(("sT", mmg[it - 2]))
        if eng == "gpsimd":
            val = emit(eng, (lambda it: lambda e: e.tensor_scalar(
                out=wgb[:, it % 2, :], in0=wgu[:, it % 2, :], scalar1=-128.0,
                scalar2=None, op0=ALU.add))(it), waits=ws, inc="sO")
            deqg[it] = ("sO", val)
            return
        val = emit(eng, (lambda it: lambda e: e.activation(
            out=wgb[:, it % 2, :], in_=wgu[:, it % 2, :], func=AF.Copy,
            bias=-128.0))(it), waits=ws, inc="sS")
        deqg[it] = ("sS", val)

    def p1_dequ(it, eng="vector"):
        ws = [("dW", wuld[it])]
        if it >= 2:
            ws.append(("sT", mmu[it - 2]))
        if eng == "gpsimd":
            val = emit(eng, (lambda it: lambda e: e.tensor_scalar(
                out=wub[:, it % 2, :], in0=wuu[:, it % 2, :], scalar1=-128.0,
                scalar2=None, op0=ALU.add))(it), waits=ws, inc="sO")
            dequ[it] = ("sO", val)
            return
        if eng == "vector":
            val = emit(eng, (lambda it: lambda e: e.tensor_scalar(
                out=wub[:, it % 2, :], in0=wuu[:, it % 2, :], scalar1=-128.0,
                scalar2=None, op0=ALU.add))(it), waits=ws, inc="sV")
            dequ[it] = ("sV", val)
        else:
            val = emit(eng, (lambda it: lambda e: e.activation(
                out=wub[:, it % 2, :], in_=wuu[:, it % 2, :], func=AF.Copy,
                bias=-128.0))(it), waits=ws, inc="sS")
            dequ[it] = ("sS", val)

    def p1_mm(it, half):
        p = it % 2
        for which in range(2):
            bankk = p if which == 0 else 2 + p
            wb = wgb if which == 0 else wub
            dq = deqg[it] if which == 0 else dequ[it]
            for hb in range(HB):
                ws = []
                if hb == 0:
                    ws = [dq] + bank_waits([bankk])
                    if it == 0 and which == 0:
                        ws.append(("sS", xnT_ready))
                    if it == NI and which == 0:
                        ws.append(("sS", evac_done[NT - 1][1]))
                tt = emit("tensor", (lambda p, wb, hb, bankk, half: lambda e: e.matmul(
                    out=psb(bankk),
                    lhsT=wb[:, p, hb * 128:(hb + 1) * 128],
                    rhs=xnT[:, hb * TOK + half * TH: hb * TOK + half * TH + TH],
                    start=(hb == 0), stop=(hb == HB - 1)))(p, wb, hb, bankk, half),
                    waits=ws, inc="sT" if hb == HB - 1 else None)
            if which == 0:
                mmg[it] = tt
            else:
                mmu[it] = tt

    def p1_evac(it):
        p = it % 2
        ab = it % 2
        i = it % NI
        ws = [("sT", mmg[it])]
        if it >= 2:
            ws.append(("sV", hmul[it - 2]))
        evA[it] = emit("scalar", (lambda p, ab, i: lambda e: e.activation(
            out=at_t[:, ab, :], in_=psb(p), func=AF.Silu,
            scale=gsc[:, i:i + 1]))(p, ab, i), waits=ws, inc="sS")
        bank_free[p] = ("sS", evA[it])
        evB[it] = emit("vector", (lambda p, ab: lambda e: e.tensor_copy(
            out=bt_t[:, ab, :], in_=psb(2 + p)))(p, ab),
            waits=[("sT", mmu[it])], inc="sV")
        bank_free[2 + p] = ("sV", evB[it])
        ws = [("sS", evA[it]), ("sV", evB[it])]
        if it >= NI:
            ws.append(("sT", mmd[NI + i]))
        else:
            if i < 32:
                s = i // 8
                ws += [("sV", norm_done[s + 4]), ("sS", sq_done[s + 4])]
            elif i < 48:
                b2 = (i - 32) // 4
                ws.append(("sT", tp0_done[b2 + 4][1]))
            elif i < 56:
                ws.append(("sS", sq_done[7]))
        hmul[it] = emit("vector", (lambda ab, i: lambda e: e.tensor_tensor(
            out=hp(i), in0=at_t[:, ab, :], in1=bt_t[:, ab, :],
            op=ALU.mult))(ab, i), waits=ws, inc="sV")

    def p2_dma(dit):
        w = (dit // NI) % 2
        i = dit % NI
        ws = [("sV", deqd[dit - 4])] if dit >= 4 else []
        wdld[dit] = emit("sync", (lambda dit, i, w: lambda e: e.dma_start(
            out=wdu[:, dit % 4, :], in_=wdT3[:, i, w * 1024:(w + 1) * 1024]))(dit, i, w),
            waits=ws, inc="dW", amt=16)

    def p2_deq(dit):
        i = dit % NI
        ws = [("dW", wdld[dit])]
        if dit >= 2:
            ws.append(("sT", mmd[dit - 2]))
        deqd[dit] = emit("vector", (lambda dit, i: lambda e: e.tensor_scalar(
            out=wdb[:, dit % 2, :], in0=wdu[:, dit % 4, :], scalar1=-128.0,
            scalar2=usc[:, i:i + 1], op0=ALU.add, op1=ALU.mult))(dit, i),
            waits=ws, inc="sV")

    def p2_mm(dit, half, i):
        q = dit % 2
        for k in range(8):
            ws = []
            if k == 0:
                ws = [("sV", deqd[dit]), ("sV", hmul[half * NI + i])]
            if i == 0:
                ws += bank_waits([k])
            tt = emit("tensor", (lambda q, k, i: lambda e: e.matmul(
                out=psb(k), lhsT=wdb[:, q, k * 128:(k + 1) * 128],
                rhs=hp(i), start=(i == 0), stop=(i == NI - 1)))(q, k, i),
                waits=ws, inc="sT" if k == 7 else None)
        mmd[dit] = tt

    exp_done = [None] * HB
    wseg_done = [None] * HB
    wd_done = [None] * HB
    attn_done = [[None, None] for _ in range(HB)]
    pmul_done = [None] * HB
    tp_done = [None] * HB
    add_done = [None] * HB
    xsl_ld = [None] * HB
    out_st = [None] * HB

    def h_freed(blocks):
        return ("sT", mmd[3 * NI + max(blocks)])

    def p3_exp(b):
        sbi = b % 5
        s1i = b % 2
        n = (b + 1) * 128
        ws = [("dW", dW_lnrep)]
        if b < 5:
            ws.append(h_freed(range(4 * sbi, 4 * sbi + 4)))
        else:
            ws.append(("sT", attn_done[b - 5][1]))
        emit("scalar", (lambda sbi, n: lambda e: e.activation(
            out=strip0(sbi)[:, 0:n], in_=eall[:, 0:n], func=AF.Exp,
            scale=lnrep[:, 0:1]))(sbi, n), waits=ws, inc="sS")
        ws = []
        if b < 2:
            ws.append(h_freed(range(20 + 4 * s1i, 24 + 4 * s1i)))
        else:
            ws.append(("sV", wseg_done[b - 2]))
        exp_done[b] = emit("scalar", (lambda s1i, n: lambda e: e.activation(
            out=strip1(s1i)[:, 0:n], in_=eall[:, 0:n], func=AF.Exp,
            scale=lnrep[:, 1:2]))(s1i, n), waits=ws, inc="sS")

    def p3_wbuild(b):
        sbi = b % 5
        nseg = b + 1
        s0_3 = strip0(sbi)[:, 0:nseg * 128].rearrange("p (s f) -> p s f", f=128)
        s1_3 = strip1(b % 2)[:, 0:nseg * 128].rearrange("p (s f) -> p s f", f=128)
        w0bc = w0rev.ap()[:, HB - 1 - b:HB].unsqueeze(2).broadcast_to((128, nseg, 128))
        w1bc = w1b.ap()[:, b * 128:(b + 1) * 128].unsqueeze(1).broadcast_to((128, nseg, 128))
        v1 = emit("vector", (lambda s0_3, w0bc: lambda e: e.tensor_tensor(
            out=s0_3, in0=s0_3, in1=w0bc, op=ALU.mult))(s0_3, w0bc),
            waits=[("sS", exp_done[b]), ("sO", 16)], inc="sV")
        v2 = emit("vector", (lambda s1_3, w1bc: lambda e: e.tensor_tensor(
            out=s1_3, in0=s1_3, in1=w1bc, op=ALU.mult))(s1_3, w1bc),
            inc="sV")
        wseg_done[b] = emit("vector", (lambda s0_3, s1_3: lambda e: e.tensor_tensor(
            out=s0_3, in0=s0_3, in1=s1_3, op=ALU.add))(s0_3, s1_3),
            waits=[("sV", v2)], inc="sV")
        wd_done[b] = emit("vector", (lambda sbi: lambda e: e.tensor_tensor(
            out=strip0(sbi)[:, 0:128], in0=strip0(sbi)[:, 0:128], in1=mt[:],
            op=ALU.mult))(sbi), waits=[("sV", wseg_done[b])], inc="sV")

    def p2_evac(half, w, wit):
        for k in range(8):
            hh = w * 8 + k
            ev = emit("scalar", (lambda k, hh, half: lambda e: e.activation(
                out=gout[:, hh * TOK + half * TH: hh * TOK + half * TH + TH],
                in_=psb(k), func=AF.Silu, scale=dsc[:, hh:hh + 1]))(k, hh, half),
                waits=[("sT", mmd[wit * NI + NI - 1])], inc="sS")
            evD[(half, w, k)] = ev
            bank_free[k] = ("sS", ev)


    norm_done = [None] * NT
    red_done = [None] * NT
    sq_done = [None] * NT
    var_done = [None] * NT
    xld = [None] * NT
    tp0_done = [[None, None] for _ in range(NT)]
    evac_done = [[None, None] for _ in range(NT)]
    xnTr = xnT.ap().rearrange("p (hb tok) -> p hb tok", tok=TOK)

    def p0_dma(t):
        ws = []
        if t >= 4:
            ws = [("sV", norm_done[t - 4])]
        xld[t] = emit("scalar", (lambda t: lambda e: e.dma_start(
            out=xbuf(t), in_=x3[:, t, :]))(t), waits=ws, inc="dX", amt=16)

    def p0_sq(t):
        b = t % 2
        ws = [("dX", xld[t])]
        if t >= 2:
            ws.append(("sV", var_done[t - 2]))
        sq_done[t] = emit("scalar", (lambda t, b: lambda e: e.activation(
            out=scr, in_=xbuf(t), func=AF.Square,
            accum_out=stats[:, b, 0:1]))(t, b), waits=ws, inc="sS")

    def p0_red(t):
        b = t % 2
        ws = [("dX", xld[t])]
        if t >= 2:
            ws.append(("sV", var_done[t - 2]))
        red_done[t] = emit("vector", (lambda t, b: lambda e: e.reduce_sum(
            out=stats[:, b, 1:2], in_=xbuf(t), axis=AX.X))(t, b),
            waits=ws, inc="sV")

    def p0_chain(t):
        b = t % 2
        ws = [("sV", red_done[t])]
        if t >= 2:
            ws.append(("sS", sqv_arr[t - 2]))
        v = emit("vector", (lambda b: lambda e: e.tensor_scalar(
            out=stats[:, b, 2:3], in0=stats[:, b, 1:2], scalar1=-1.0 / D,
            scalar2=None, op0=ALU.mult))(b),
            waits=ws, inc="sV")
        v = emit("vector", (lambda b: lambda e: e.tensor_tensor(
            out=stats[:, b, 3:4], in0=stats[:, b, 2:3], in1=stats[:, b, 2:3],
            op=ALU.mult))(b), waits=[("sV", v)], inc="sV")
        var_done[t] = emit("vector", (lambda b: lambda e: e.scalar_tensor_tensor(
            out=stats[:, b, 4:5], in0=stats[:, b, 0:1], scalar=1.0 / D,
            in1=stats[:, b, 3:4], op0=ALU.mult, op1=ALU.subtract))(b),
            waits=[("sV", v), ("sS", sq_done[t])], inc="sV")
        veps = emit("vector", (lambda b: lambda e: e.tensor_scalar(
            out=stats[:, b, 3:4], in0=stats[:, b, 4:5], scalar1=1e-5,
            scalar2=None, op0=ALU.add))(b), waits=[("sV", var_done[t])], inc="sV")
        return veps

    def p0_sqrt(t, veps):
        b = t % 2
        return emit("scalar", (lambda b: lambda e: e.activation(
            out=stats[:, b, 5:6], in_=stats[:, b, 3:4], func=AF.Sqrt))(b),
            waits=[("sV", veps)], inc="sS")

    def p0_norm(t, sqv):
        b = t % 2
        rcp = emit("vector", (lambda b: lambda e: e.reciprocal(
            out=stats[:, b, 6:7], in_=stats[:, b, 5:6]))(b),
            waits=[("sS", sqv)], inc="sV")
        ws = [("sV", rcp)]
        if t >= 4:
            ws.append(("sT", tp0_done[t - 4][1]))
        norm_done[t] = emit("vector", (lambda t, b: lambda e: e.tensor_scalar(
            out=xnt_t(t), in0=xbuf(t), scalar1=stats[:, b, 2:3],
            scalar2=stats[:, b, 6:7], op0=ALU.add, op1=ALU.mult))(t, b),
            waits=ws, inc="sV")

    def p0_transp(t):
        for k2 in range(2):
            bank = (2 * t + k2) % 4 if t < 4 else 4 + (2 * t + k2) % 4
            for j in range(8):
                hb = k2 * 8 + j
                ws = []
                if j == 0:
                    ws = [("sV", norm_done[t]), ("dW", cl_done["idb"])]
                    ws += bank_waits([bank])
                tt = emit("tensor", (lambda t, bank, j, hb: lambda e: e.transpose(
                    out=psb_bf(bank)[:, j * 128:(j + 1) * 128],
                    in_=xnt_t(t)[:, hb * 128:(hb + 1) * 128],
                    identity=idb[:]))(t, bank, j, hb),
                    waits=ws, inc="sT" if j == 7 else None)
                if j == 7:
                    tp0_done[t][k2] = tt

    def p0_evac(t):
        for k2 in range(2):
            bank = (2 * t + k2) % 4 if t < 4 else 4 + (2 * t + k2) % 4
            ev = emit("scalar", (lambda bank, k2, t: lambda e: e.activation(
                out=xnTr[:, k2 * 8:(k2 + 1) * 8, t * 128:(t + 1) * 128],
                in_=psb_bf(bank).rearrange("p (j f) -> p j f", f=128),
                func=AF.Copy))(bank, k2, t),
                waits=[("sT", tp0_done[t][k2])], inc="sS")
            evac_done[t][k2] = ev
            bank_free[bank] = ("sS", ev)

    sqv_arr = [None] * NT
    p0_dma(0)
    p0_dma(1)
    p0_sq(0)
    p0_red(0)
    p0_sq(1)
    p0_red(1)
    sqv_arr[0] = p0_sqrt(0, p0_chain(0))
    for t in range(NT):
        if t + 2 < NT:
            p0_dma(t + 2)
        if t + 1 < NT:
            sqv_arr[t + 1] = p0_sqrt(t + 1, p0_chain(t + 1))
        p0_norm(t, sqv_arr[t])
        if t == 1:
            p1_dma(0)
            p1_dma(1)
        if t == 3:
            p1_deqg(0)
            p1_deqg(1)
            p1_dequ(0)
            p1_dequ(1)
        if 1 <= t <= 4:
            p0_evac(t - 1)
        if t + 2 < NT:
            p0_sq(t + 2)
            p0_red(t + 2)
        if t <= 3:
            p0_transp(t)
    emit("scalar", lambda e: e.preload_act_table(AF.Silu))

    xnT_ready = evac_done[3][1]

    for half in range(2):
        base = half * NI
        for i in range(NI):
            it = base + i
            if i + 2 < NI:
                p1_dma(it + 2)
            nxt = it + 1
            if i + 1 < NI and not (nxt <= base + 1):
                p1_deqg(nxt)
                p1_dequ(nxt)
            p1_mm(it, half)
            if half == 0 and i < 4:
                p0_transp(4 + i)
            if half == 0 and 1 <= i <= 4:
                p0_evac(3 + i)
            p1_evac(it)
        dbase0 = half * 2 * NI
        for pf in range(4):
            p2_dma(dbase0 + pf)
        p2_deq(dbase0)
        p2_deq(dbase0 + 1)
        if half == 0:
            p1_dma(NI)
            p1_dma(NI + 1)
            p1_deqg(NI)
            p1_dequ(NI, eng="scalar")
            p1_deqg(NI + 1)
            p1_dequ(NI + 1, eng="scalar")
        for w in range(2):
            wit = half * 2 + w
            dbase = wit * NI
            for i in range(NI):
                dit = dbase + i
                if dit >= dbase0 + 2:
                    p2_deq(dit)
                if dit + 4 < dbase0 + 2 * NI:
                    p2_dma(dit + 4)
                p2_mm(dit, half, i)
                if half == 1 and w == 1:
                    if i == 28:
                        emit("scalar", lambda e: e.preload_act_table(AF.Exp))
                        p3_exp(0)
                    elif i == 30:
                        p3_exp(1)
                    elif i == 44:
                        p3_wbuild(0)
                    elif i == 46:
                        p3_exp(2)
                    elif i == 48:
                        p3_wbuild(1)
                    elif i == 50:
                        p3_exp(3)
                    elif i == 52:
                        p3_wbuild(2)
                    elif i == 54:
                        p3_exp(4)
                    elif i == 56:
                        p3_wbuild(3)
                    elif i == 60:
                        p3_wbuild(4)
            if half == 1 and w == 1:
                emit("scalar", lambda e: e.preload_act_table(AF.Silu))
            p2_evac(half, w, wit)
            if half == 1 and w == 1:
                emit("scalar", lambda e: e.preload_act_table(AF.Exp))

    def p3_xsl(b):
        pb = b % 2
        if b < 2:
            ws = [h_freed(range(46 + pb * 4, 50 + pb * 4))]
        else:
            ws = [("sV", add_done[b - 2])]
        xsl_ld[b] = emit("sync", (lambda b, pb: lambda e: e.dma_start(
            out=xslv(pb).rearrange("p (j f) -> p j f", f=128),
            in_=x3[:, :, b * 128:(b + 1) * 128]))(b, pb), waits=ws, inc="dW", amt=16)

    def p3_attn(b):
        sbi = b % 5
        pb = b % 2
        for th in range(2):
            pa = pb * 2 + th
            for a in range(b + 1):
                d = b - a
                ws = []
                if a == 0:
                    ws = [("sV", wd_done[b])] + bank_waits([pa])
                tt = emit("tensor", (lambda sbi, pa, a, d, th: lambda e: e.matmul(
                    out=psb(pa),
                    lhsT=strip0(sbi)[:, d * 128:(d + 1) * 128],
                    rhs=xnT[:, a * TOK + th * TH: a * TOK + th * TH + TH],
                    start=(a == 0), stop=(a == b)))(sbi, pa, a, d, th),
                    waits=ws, inc="sT" if a == b else None)
            attn_done[b][th] = tt

    def p3_pmul(b):
        pb = b % 2
        ws = [("sT", attn_done[b][1]),
              ("sS", evD[(1, b // 8, b % 8)])]
        if b >= 2:
            ws.append(("sT", tp_done[b - 2]))
        pmul_done[b] = emit("vector", (lambda pb, b: lambda e: e.scalar_tensor_tensor(
            out=pTv(pb).rearrange("p (t f) -> p t f", f=512),
            in0=psap[:, pb * 2:pb * 2 + 2, :],
            scalar=biasr[:, b:b + 1],
            in1=gout[:, b * TOK:(b + 1) * TOK].rearrange("p (t f) -> p t f", f=512),
            op0=ALU.add, op1=ALU.mult))(pb, b), waits=ws, inc="sV")
        bank_free[pb * 2] = ("sV", pmul_done[b])
        bank_free[pb * 2 + 1] = ("sV", pmul_done[b])

    def p3_transp(b):
        pb = b % 2
        for j in range(8):
            bank = 4 + 2 * pb + j // 4
            ws = []
            if j == 0:
                ws = [("sV", pmul_done[b])] + bank_waits([4 + 2 * pb, 5 + 2 * pb])
            tt = emit("tensor", (lambda pb, j, bank: lambda e: e.transpose(
                out=psap[:, bank, (j % 4) * 128:(j % 4) * 128 + 128],
                in_=pTv(pb)[:, j * 128:(j + 1) * 128],
                identity=idf[:]))(pb, j, bank),
                waits=ws, inc="sT" if j == 7 else None)
        tp_done[b] = tt

    def p3_add(b):
        pb = b % 2
        ws = [("sT", tp_done[b]), ("dW", xsl_ld[b])]
        if b < 2:
            ws.append(h_freed(range(54 + pb * 4, 58 + pb * 4)))
        if b >= 2:
            ws.append(("sO", out_st[b - 2]))
        add_done[b] = emit("vector", (lambda pb: lambda e: e.tensor_tensor(
            out=obufv(pb).rearrange("p (t f) -> p t f", f=512),
            in0=psap[:, 4 + 2 * pb:6 + 2 * pb, :],
            in1=xslv(pb).rearrange("p (t f) -> p t f", f=512),
            op=ALU.add))(pb), waits=ws, inc="sV")
        bank_free[4 + 2 * pb] = ("sV", add_done[b])
        bank_free[5 + 2 * pb] = ("sV", add_done[b])

    def p3_store(b):
        pb = b % 2
        out_st[b] = emit("gpsimd", (lambda pb, b: lambda e: e.dma_start(
            out=y3[:, :, b * 128:(b + 1) * 128],
            in_=obufv(pb).rearrange("p (j f) -> p j f", f=128)))(pb, b),
            waits=[("sV", add_done[b])], inc="sO", amt=16)

    for b in range(HB):
        p3_attn(b)
        if b + 5 < HB:
            p3_exp(b + 5)
        if b >= 2:
            p3_transp(b - 2)
        p3_pmul(b)
        if b >= 2:
            p3_add(b - 2)
        p3_xsl(b)
        if b + 5 < HB:
            p3_wbuild(b + 5)
        if b >= 2:
            p3_store(b - 2)
    for b in (HB - 2, HB - 1):
        p3_transp(b)
        p3_add(b)
        p3_store(b)
    emit("gpsimd", lambda e: e.nop(), waits=[("sO", semv["sO"])])

    sems = {}
    for name in ("dW", "dX", "sT", "sS", "sV", "sO"):
        sems[name] = ctx.enter_context(nc.semaphore(name))
    block = ctx.enter_context(nc.Block())

    def runner(engname):
        def _run(eng):
            for ws, fn, inc, amt in steps[engname]:
                for (s, vv) in ws:
                    eng.wait_ge(sems[s], vv)
                ins = fn(eng)
                if inc is not None:
                    ins.then_inc(sems[inc], amt)
        return _run

    block.sync(runner("sync"))
    block.scalar(runner("scalar"))
    block.vector(runner("vector"))
    block.tensor(runner("tensor"))
    block.gpsimd(runner("gpsimd"))
    ctx.close()
    return nc


def _low_byte(w):
    w = np.ascontiguousarray(np.asarray(w, dtype=np.int32))
    assert np.little_endian
    return w.view(np.uint8).reshape(w.shape[0], w.shape[1], 4)[:, :, 0]


def kernel(L0_attn_x, L0_attn_weight, L0_attn_bias, L0_attn_decay_value,
           L0_ffn_gate_w, L0_ffn_gate_scale, L0_ffn_up_w, L0_ffn_up_scale,
           L0_ffn_down_w, L0_ffn_down_scale):
    x = np.ascontiguousarray(np.asarray(L0_attn_x, dtype=np.float32).reshape(SEQ, D))
    wgT = np.ascontiguousarray(_low_byte(L0_ffn_gate_w).T)
    wuT = np.ascontiguousarray(_low_byte(L0_ffn_up_w).T)
    wdT = np.ascontiguousarray(_low_byte(L0_ffn_down_w).T)
    gsc = np.ascontiguousarray(np.asarray(L0_ffn_gate_scale, np.float32).reshape(NI, 128).T)
    usc = np.ascontiguousarray(np.asarray(L0_ffn_up_scale, np.float32).reshape(NI, 128).T)
    dsc = np.ascontiguousarray(np.asarray(L0_ffn_down_scale, np.float32).reshape(HB, 128).T)
    biasr = np.ascontiguousarray(np.asarray(L0_attn_bias, np.float32).reshape(HB, 128).T)
    wgt = np.asarray(L0_attn_weight, np.float32)
    w0r = np.ascontiguousarray(wgt[0].reshape(HB, 128).T)
    w0rev = np.ascontiguousarray(w0r[:, ::-1])
    w1row = np.ascontiguousarray(wgt[1].reshape(1, D))
    dvrow = np.ascontiguousarray(np.asarray(L0_attn_decay_value, np.float32).reshape(1, 2))
    eall = (np.arange(D)[None, :] - np.arange(128)[:, None]).astype(np.float16)
    mtile = (np.arange(128)[None, :] >= np.arange(128)[:, None]).astype(ml_dtypes.bfloat16)
    idbf = np.eye(128, dtype=ml_dtypes.bfloat16)
    idf32 = np.eye(128, dtype=np.float32)

    shared = {"wgT": wgT, "wuT": wuT, "wdT": wdT, "gsc": gsc, "usc": usc,
              "dsc": dsc, "biasr": biasr, "w0r": w0r, "w0rev": w0rev, "w1row": w1row,
              "dvrow": dvrow, "eall": eall, "mtile": mtile, "idbf": idbf,
              "idf32": idf32}
    in_maps = []
    for c in range(NCORE):
        m = dict(shared)
        m["x"] = np.ascontiguousarray(x[c * TOK:(c + 1) * TOK])
        in_maps.append(m)

    nc = build_program()
    res = run_bass_kernel_spmd(nc, in_maps, core_ids=list(range(NCORE)))
    y = np.concatenate([res.results[c]["y"] for c in range(NCORE)], axis=0)
    return y.reshape(1, SEQ, D)


# version 15
# speedup vs baseline: 1.1900x; 1.1900x over previous
import numpy as np
import ml_dtypes
from contextlib import ExitStack

import concourse.bass as bass
import concourse.mybir as mybir
from concourse.bass_utils import run_bass_kernel_spmd

F32 = mybir.dt.float32
F16 = mybir.dt.float16
BF16 = mybir.dt.bfloat16
U8 = mybir.dt.uint8
AF = mybir.ActivationFunctionType
ALU = mybir.AluOpType
AX = mybir.AxisListType

D = 2048
SEQ = 8192
NCORE = 8
TOK = SEQ // NCORE
NT = TOK // 128
HB = D // 128
NI = 64
TH = 512


def build_program():
    nc = bass.Bass("TRN2", target_bir_lowering=False)

    x_d = nc.dram_tensor("x", [TOK, D], F32, kind="ExternalInput")
    wgT_d = nc.dram_tensor("wgT", [D, 8192], U8, kind="ExternalInput")
    wuT_d = nc.dram_tensor("wuT", [D, 8192], U8, kind="ExternalInput")
    wdT_d = nc.dram_tensor("wdT", [8192, D], U8, kind="ExternalInput")
    gsc_d = nc.dram_tensor("gsc", [128, NI], F32, kind="ExternalInput")
    usc_d = nc.dram_tensor("usc", [128, NI], F32, kind="ExternalInput")
    dsc_d = nc.dram_tensor("dsc", [128, HB], F32, kind="ExternalInput")
    bias_d = nc.dram_tensor("biasr", [128, HB], F32, kind="ExternalInput")
    w0_d = nc.dram_tensor("w0r", [128, HB], F32, kind="ExternalInput")
    w0rev_d = nc.dram_tensor("w0rev", [128, HB], F32, kind="ExternalInput")
    w1_d = nc.dram_tensor("w1row", [1, D], F32, kind="ExternalInput")
    dv_d = nc.dram_tensor("dvrow", [1, 2], F32, kind="ExternalInput")
    eall_d = nc.dram_tensor("eall", [128, D], F16, kind="ExternalInput")
    mt_d = nc.dram_tensor("mtile", [128, 128], BF16, kind="ExternalInput")
    idb_d = nc.dram_tensor("idbf", [128, 128], BF16, kind="ExternalInput")
    idf_d = nc.dram_tensor("idf32", [128, 128], F32, kind="ExternalInput")
    lnscr_d = nc.dram_tensor("lnscr", [1, 2], F32, kind="Internal")
    y_d = nc.dram_tensor("y", [TOK, D], F32, kind="ExternalOutput")

    x3 = x_d.ap().rearrange("(j p) f -> p j f", p=128)
    y3 = y_d.ap().rearrange("(j p) f -> p j f", p=128)
    wgT3 = wgT_d.ap().rearrange("(hb p) i -> p hb i", p=128)
    wuT3 = wuT_d.ap().rearrange("(hb p) i -> p hb i", p=128)
    wdT3 = wdT_d.ap().rearrange("(ib p) h -> p ib h", p=128)

    ctx = ExitStack()

    def sb(name, shape, dt):
        return ctx.enter_context(nc.sbuf_tensor(name, shape, dt))

    xnT = sb("xnT", [128, HB * TOK], BF16)
    gout = sb("gout", [128, HB * TOK], F32)
    arena = sb("arena", [128, 65536], U8)
    wgu = sb("wgu", [128, 2, D], U8)
    wgb = sb("wgb", [128, 2, D], BF16)
    wuu = sb("wuu", [128, 2, D], U8)
    wub = sb("wub", [128, 2, D], BF16)
    wdu = sb("wdu", [128, 4, 1024], U8)
    wdb = sb("wdb", [128, 2, 1024], BF16)
    at_t = sb("at", [128, 2, TH], BF16)
    bt_t = sb("bt", [128, 2, TH], BF16)
    eall = sb("ealls", [128, D], F16)
    w1b = sb("w1b", [128, D], BF16)
    mt = sb("mt", [128, 128], BF16)
    idb = sb("idb", [128, 128], BF16)
    idf = sb("idf", [128, 128], F32)
    gsc = sb("gscs", [128, NI], F32)
    usc = sb("uscs", [128, NI], F32)
    dsc = sb("dscs", [128, HB], F32)
    biasr = sb("biasrs", [128, HB], F32)
    w0r = sb("w0rs", [128, HB], F32)
    w0rev = sb("w0revs", [128, HB], F32)
    dvs = sb("dvs", [1, 2], F32)
    dvc = sb("dvc", [1, 2], F32)
    lndv = sb("lndv", [1, 2], F32)
    lnrep = sb("lnrep", [128, 2], F32)
    stats = sb("stats", [128, 2, 8], F32)

    PS = ctx.enter_context(nc.psum_tensor("PS", [128, 8, 512], F32))

    A = arena.ap()

    def xbuf(t):
        b = t % 4
        return A[:, b * 8192:(b + 1) * 8192].bitcast(F32)

    def xnt_t(t):
        b = t % 4
        return A[:, 32768 + b * 4096:32768 + (b + 1) * 4096].bitcast(BF16)

    scr = A[:, 49152:57344].bitcast(F32)

    hp_all = A[:, 0:65536].bitcast(BF16)

    def hp(i):
        return hp_all[:, i * TH:(i + 1) * TH]

    def strip0(sbi):
        return A[:, sbi * 4096:(sbi + 1) * 4096].bitcast(BF16)

    def strip1(sbi):
        return A[:, 28672 + sbi * 4096:28672 + (sbi + 1) * 4096].bitcast(BF16)

    def pTv(pb):
        return A[:, 38912 + pb * 4096:38912 + (pb + 1) * 4096].bitcast(F32)

    def xslv(pb):
        return A[:, 47104 + pb * 4096:47104 + (pb + 1) * 4096].bitcast(F32)

    def obufv(pb):
        return A[:, 55296 + pb * 4096:55296 + (pb + 1) * 4096].bitcast(F32)

    psap = PS.ap()

    def psb(k):
        return psap[:, k, :]

    def psb_bf(k):
        return psap[:, k, :].bitcast(BF16)

    ENGS = ("sync", "scalar", "vector", "tensor", "gpsimd")
    steps = {e: [] for e in ENGS}
    semv = {s: 0 for s in ("dW", "dX", "sT", "sS", "sV", "sO")}

    def emit(eng, fn, waits=(), inc=None, amt=1):
        for (s, v) in waits:
            assert v is not None, "wait references an un-emitted milestone"
        ws = tuple((s, v) for (s, v) in waits if v > 0)
        steps[eng].append((ws, fn, inc, amt))
        if inc is not None:
            semv[inc] += amt
        return semv[inc] if inc is not None else 0

    bank_free = [None] * 8

    def bank_waits(banks):
        ws = []
        for k in banks:
            if bank_free[k] is not None and bank_free[k] not in ws:
                ws.append(bank_free[k])
        return ws

    cl = [(dvs, dv_d), (idb, idb_d), (gsc, gsc_d), (usc, usc_d), (dsc, dsc_d),
          (biasr, bias_d), (w0r, w0_d), (w0rev, w0rev_d), (idf, idf_d), (mt, mt_d), (eall, eall_d)]
    cl_done = {}
    for dst, src in cl:
        dW_consts = emit(
            "sync",
            (lambda dst, src: lambda e: e.dma_start(out=dst[:], in_=src.ap()))(dst, src),
            inc="dW", amt=16)
        cl_done[dst.name] = dW_consts

    v = emit("vector", lambda e: e.tensor_scalar(
        out=dvc[:], in0=dvs[:], scalar1=0.9, scalar2=None, op0=ALU.max),
        waits=[("dW", cl_done["dvs"])], inc="sV")
    v = emit("vector", lambda e: e.tensor_scalar(
        out=dvc[:], in0=dvc[:], scalar1=1.0, scalar2=None, op0=ALU.min),
        waits=[("sV", v)], inc="sV")
    emit("scalar", lambda e: e.preload_act_table(AF.Ln))
    sS_ln = emit("scalar", lambda e: e.activation(out=lndv[:], in_=dvc[:], func=AF.Ln),
                 waits=[("sV", v)], inc="sS")
    emit("scalar", lambda e: e.preload_act_table(AF.Sqrt))
    dw = emit("sync", lambda e: e.dma_start(out=lnscr_d.ap(), in_=lndv[:]),
              waits=[("sS", sS_ln)], inc="dW", amt=16)
    dW_lnrep = emit("sync", lambda e: e.dma_start(
        out=lnrep[:], in_=lnscr_d.ap().broadcast_to((128, 2))),
        waits=[("dW", dw)], inc="dW", amt=16)

    emit("gpsimd", lambda e: e.dma_start(
        out=w1b[:], in_=w1_d.ap().broadcast_to((128, D))), inc="sO", amt=16)

    NIT = 2 * NI
    wgld = [None] * NIT
    wuld = [None] * NIT
    deqg = [None] * NIT
    dequ = [None] * NIT
    mmg = [None] * NIT
    mmu = [None] * NIT
    evA = [None] * NIT
    evB = [None] * NIT
    hmul = [None] * NIT
    NDIT = 4 * NI
    wdld = [None] * NDIT
    deqd = [None] * NDIT
    mmd = [None] * NDIT
    evD = {}

    def p1_dma(it):
        i = it % NI
        ws = [deqg[it - 2]] if it >= 2 else []
        wgld[it] = emit("sync", (lambda it, i: lambda e: e.dma_start(
            out=wgu[:, it % 2, :], in_=wgT3[:, :, i * 128:(i + 1) * 128]))(it, i),
            waits=ws, inc="dW", amt=16)
        ws = [dequ[it - 2]] if it >= 2 else []
        wuld[it] = emit("sync", (lambda it, i: lambda e: e.dma_start(
            out=wuu[:, it % 2, :], in_=wuT3[:, :, i * 128:(i + 1) * 128]))(it, i),
            waits=ws, inc="dW", amt=16)

    def p1_deqg(it, eng="scalar"):
        ws = [("dW", wgld[it])]
        if it >= 2:
            ws.append(("sT", mmg[it - 2]))
        if eng == "gpsimd":
            val = emit(eng, (lambda it: lambda e: e.tensor_scalar(
                out=wgb[:, it % 2, :], in0=wgu[:, it % 2, :], scalar1=-128.0,
                scalar2=None, op0=ALU.add))(it), waits=ws, inc="sO")
            deqg[it] = ("sO", val)
            return
        val = emit(eng, (lambda it: lambda e: e.activation(
            out=wgb[:, it % 2, :], in_=wgu[:, it % 2, :], func=AF.Copy,
            bias=-128.0))(it), waits=ws, inc="sS")
        deqg[it] = ("sS", val)

    def p1_dequ(it, eng="vector"):
        ws = [("dW", wuld[it])]
        if it >= 2:
            ws.append(("sT", mmu[it - 2]))
        if eng == "gpsimd":
            val = emit(eng, (lambda it: lambda e: e.tensor_scalar(
                out=wub[:, it % 2, :], in0=wuu[:, it % 2, :], scalar1=-128.0,
                scalar2=None, op0=ALU.add))(it), waits=ws, inc="sO")
            dequ[it] = ("sO", val)
            return
        if eng == "vector":
            val = emit(eng, (lambda it: lambda e: e.tensor_scalar(
                out=wub[:, it % 2, :], in0=wuu[:, it % 2, :], scalar1=-128.0,
                scalar2=None, op0=ALU.add))(it), waits=ws, inc="sV")
            dequ[it] = ("sV", val)
        else:
            val = emit(eng, (lambda it: lambda e: e.activation(
                out=wub[:, it % 2, :], in_=wuu[:, it % 2, :], func=AF.Copy,
                bias=-128.0))(it), waits=ws, inc="sS")
            dequ[it] = ("sS", val)

    def p1_mm(it, half):
        p = it % 2
        for which in range(2):
            bankk = p if which == 0 else 2 + p
            wb = wgb if which == 0 else wub
            dq = deqg[it] if which == 0 else dequ[it]
            for hb in range(HB):
                ws = []
                if hb == 0:
                    ws = [dq] + bank_waits([bankk])
                    if it == 0 and which == 0:
                        ws.append(("sS", xnT_ready))
                    if it == NI and which == 0:
                        ws.append(("sS", evac_done[NT - 1][1]))
                tt = emit("tensor", (lambda p, wb, hb, bankk, half: lambda e: e.matmul(
                    out=psb(bankk),
                    lhsT=wb[:, p, hb * 128:(hb + 1) * 128],
                    rhs=xnT[:, hb * TOK + half * TH: hb * TOK + half * TH + TH],
                    start=(hb == 0), stop=(hb == HB - 1)))(p, wb, hb, bankk, half),
                    waits=ws, inc="sT" if hb == HB - 1 else None)
            if which == 0:
                mmg[it] = tt
            else:
                mmu[it] = tt

    def p1_evac(it):
        p = it % 2
        ab = it % 2
        i = it % NI
        ws = [("sT", mmg[it])]
        if it >= 2:
            ws.append(("sV", hmul[it - 2]))
        evA[it] = emit("scalar", (lambda p, ab, i: lambda e: e.activation(
            out=at_t[:, ab, :], in_=psb(p), func=AF.Silu,
            scale=gsc[:, i:i + 1]))(p, ab, i), waits=ws, inc="sS")
        bank_free[p] = ("sS", evA[it])
        evB[it] = emit("vector", (lambda p, ab: lambda e: e.tensor_copy(
            out=bt_t[:, ab, :], in_=psb(2 + p)))(p, ab),
            waits=[("sT", mmu[it])], inc="sV")
        bank_free[2 + p] = ("sV", evB[it])
        ws = [("sS", evA[it]), ("sV", evB[it])]
        if it >= NI:
            ws.append(("sT", mmd[NI + i]))
        else:
            if i < 32:
                s = i // 8
                ws += [("sV", norm_done[s + 4]), ("sS", sq_done[s + 4])]
            elif i < 48:
                b2 = (i - 32) // 4
                ws.append(("sT", tp0_done[b2 + 4][1]))
            elif i < 56:
                ws.append(("sS", sq_done[7]))
        hmul[it] = emit("vector", (lambda ab, i: lambda e: e.tensor_tensor(
            out=hp(i), in0=at_t[:, ab, :], in1=bt_t[:, ab, :],
            op=ALU.mult))(ab, i), waits=ws, inc="sV")

    def p2_dma(dit):
        w = (dit // NI) % 2
        i = dit % NI
        ws = [("sV", deqd[dit - 4])] if dit >= 4 else []
        wdld[dit] = emit("sync", (lambda dit, i, w: lambda e: e.dma_start(
            out=wdu[:, dit % 4, :], in_=wdT3[:, i, w * 1024:(w + 1) * 1024]))(dit, i, w),
            waits=ws, inc="dW", amt=16)

    def p2_deq(dit):
        i = dit % NI
        ws = [("dW", wdld[dit])]
        if dit >= 2:
            ws.append(("sT", mmd[dit - 2]))
        deqd[dit] = emit("vector", (lambda dit, i: lambda e: e.tensor_scalar(
            out=wdb[:, dit % 2, :], in0=wdu[:, dit % 4, :], scalar1=-128.0,
            scalar2=usc[:, i:i + 1], op0=ALU.add, op1=ALU.mult))(dit, i),
            waits=ws, inc="sV")

    def p2_mm(dit, half, i):
        q = dit % 2
        for k in range(8):
            ws = []
            if k == 0:
                ws = [("sV", deqd[dit]), ("sV", hmul[half * NI + i])]
            if i == 0:
                ws += bank_waits([k])
            tt = emit("tensor", (lambda q, k, i: lambda e: e.matmul(
                out=psb(k), lhsT=wdb[:, q, k * 128:(k + 1) * 128],
                rhs=hp(i), start=(i == 0), stop=(i == NI - 1)))(q, k, i),
                waits=ws, inc="sT" if k == 7 else None)
        mmd[dit] = tt

    exp_done = [None] * HB
    wseg_done = [None] * HB
    wd_done = [None] * HB
    attn_done = [[None, None] for _ in range(HB)]
    pmul_done = [None] * HB
    tp_done = [None] * HB
    add_done = [None] * HB
    xsl_ld = [None] * HB
    out_st = [None] * HB

    def h_freed(blocks):
        return ("sT", mmd[3 * NI + max(blocks)])

    def p3_exp(b):
        sbi = b % 7
        s1i = b % 2
        n = (b + 1) * 128
        ws = [("dW", dW_lnrep)]
        if b < 7:
            ws.append(h_freed(range(4 * sbi, 4 * sbi + 4)))
        else:
            ws.append(("sT", attn_done[b - 7][1]))
        emit("scalar", (lambda sbi, n: lambda e: e.activation(
            out=strip0(sbi)[:, 0:n], in_=eall[:, 0:n], func=AF.Exp,
            scale=lnrep[:, 0:1]))(sbi, n), waits=ws, inc="sS")
        ws = []
        if b < 2:
            ws.append(h_freed(range(28 + 4 * s1i, 32 + 4 * s1i)))
        else:
            ws.append(("sV", wseg_done[b - 2]))
        exp_done[b] = emit("scalar", (lambda s1i, n: lambda e: e.activation(
            out=strip1(s1i)[:, 0:n], in_=eall[:, 0:n], func=AF.Exp,
            scale=lnrep[:, 1:2]))(s1i, n), waits=ws, inc="sS")

    def p3_wbuild(b):
        sbi = b % 7
        nseg = b + 1
        s0_3 = strip0(sbi)[:, 0:nseg * 128].rearrange("p (s f) -> p s f", f=128)
        s1_3 = strip1(b % 2)[:, 0:nseg * 128].rearrange("p (s f) -> p s f", f=128)
        w0bc = w0rev.ap()[:, HB - 1 - b:HB].unsqueeze(2).broadcast_to((128, nseg, 128))
        w1bc = w1b.ap()[:, b * 128:(b + 1) * 128].unsqueeze(1).broadcast_to((128, nseg, 128))
        v1 = emit("vector", (lambda s0_3, w0bc: lambda e: e.tensor_tensor(
            out=s0_3, in0=s0_3, in1=w0bc, op=ALU.mult))(s0_3, w0bc),
            waits=[("sS", exp_done[b]), ("sO", 16)], inc="sV")
        v2 = emit("vector", (lambda s1_3, w1bc: lambda e: e.tensor_tensor(
            out=s1_3, in0=s1_3, in1=w1bc, op=ALU.mult))(s1_3, w1bc),
            inc="sV")
        wseg_done[b] = emit("vector", (lambda s0_3, s1_3: lambda e: e.tensor_tensor(
            out=s0_3, in0=s0_3, in1=s1_3, op=ALU.add))(s0_3, s1_3),
            waits=[("sV", v2)], inc="sV")
        wd_done[b] = emit("vector", (lambda sbi: lambda e: e.tensor_tensor(
            out=strip0(sbi)[:, 0:128], in0=strip0(sbi)[:, 0:128], in1=mt[:],
            op=ALU.mult))(sbi), waits=[("sV", wseg_done[b])], inc="sV")

    def p2_evac(half, w, wit):
        for k in range(8):
            hh = w * 8 + k
            ev = emit("scalar", (lambda k, hh, half: lambda e: e.activation(
                out=gout[:, hh * TOK + half * TH: hh * TOK + half * TH + TH],
                in_=psb(k), func=AF.Silu, scale=dsc[:, hh:hh + 1]))(k, hh, half),
                waits=[("sT", mmd[wit * NI + NI - 1])], inc="sS")
            evD[(half, w, k)] = ev
            bank_free[k] = ("sS", ev)


    norm_done = [None] * NT
    red_done = [None] * NT
    sq_done = [None] * NT
    var_done = [None] * NT
    xld = [None] * NT
    tp0_done = [[None, None] for _ in range(NT)]
    evac_done = [[None, None] for _ in range(NT)]
    xnTr = xnT.ap().rearrange("p (hb tok) -> p hb tok", tok=TOK)

    def p0_dma(t):
        ws = []
        if t >= 4:
            ws = [("sV", norm_done[t - 4])]
        xld[t] = emit("scalar", (lambda t: lambda e: e.dma_start(
            out=xbuf(t), in_=x3[:, t, :]))(t), waits=ws, inc="dX", amt=16)

    def p0_sq(t):
        b = t % 2
        ws = [("dX", xld[t])]
        if t >= 2:
            ws.append(("sV", var_done[t - 2]))
        sq_done[t] = emit("scalar", (lambda t, b: lambda e: e.activation(
            out=scr, in_=xbuf(t), func=AF.Square,
            accum_out=stats[:, b, 0:1]))(t, b), waits=ws, inc="sS")

    def p0_red(t):
        b = t % 2
        ws = [("dX", xld[t])]
        if t >= 2:
            ws.append(("sV", var_done[t - 2]))
        red_done[t] = emit("vector", (lambda t, b: lambda e: e.reduce_sum(
            out=stats[:, b, 1:2], in_=xbuf(t), axis=AX.X))(t, b),
            waits=ws, inc="sV")

    def p0_chain(t):
        b = t % 2
        ws = [("sV", red_done[t])]
        if t >= 2:
            ws.append(("sS", sqv_arr[t - 2]))
        v = emit("vector", (lambda b: lambda e: e.tensor_scalar(
            out=stats[:, b, 2:3], in0=stats[:, b, 1:2], scalar1=-1.0 / D,
            scalar2=None, op0=ALU.mult))(b),
            waits=ws, inc="sV")
        v = emit("vector", (lambda b: lambda e: e.tensor_tensor(
            out=stats[:, b, 3:4], in0=stats[:, b, 2:3], in1=stats[:, b, 2:3],
            op=ALU.mult))(b), waits=[("sV", v)], inc="sV")
        var_done[t] = emit("vector", (lambda b: lambda e: e.scalar_tensor_tensor(
            out=stats[:, b, 4:5], in0=stats[:, b, 0:1], scalar=1.0 / D,
            in1=stats[:, b, 3:4], op0=ALU.mult, op1=ALU.subtract))(b),
            waits=[("sV", v), ("sS", sq_done[t])], inc="sV")
        veps = emit("vector", (lambda b: lambda e: e.tensor_scalar(
            out=stats[:, b, 3:4], in0=stats[:, b, 4:5], scalar1=1e-5,
            scalar2=None, op0=ALU.add))(b), waits=[("sV", var_done[t])], inc="sV")
        return veps

    def p0_sqrt(t, veps):
        b = t % 2
        return emit("scalar", (lambda b: lambda e: e.activation(
            out=stats[:, b, 5:6], in_=stats[:, b, 3:4], func=AF.Sqrt))(b),
            waits=[("sV", veps)], inc="sS")

    def p0_norm(t, sqv):
        b = t % 2
        rcp = emit("vector", (lambda b: lambda e: e.reciprocal(
            out=stats[:, b, 6:7], in_=stats[:, b, 5:6]))(b),
            waits=[("sS", sqv)], inc="sV")
        ws = [("sV", rcp)]
        if t >= 4:
            ws.append(("sT", tp0_done[t - 4][1]))
        norm_done[t] = emit("vector", (lambda t, b: lambda e: e.tensor_scalar(
            out=xnt_t(t), in0=xbuf(t), scalar1=stats[:, b, 2:3],
            scalar2=stats[:, b, 6:7], op0=ALU.add, op1=ALU.mult))(t, b),
            waits=ws, inc="sV")

    def p0_transp(t):
        for k2 in range(2):
            bank = (2 * t + k2) % 4 if t < 4 else 4 + (2 * t + k2) % 4
            for j in range(8):
                hb = k2 * 8 + j
                ws = []
                if j == 0:
                    ws = [("sV", norm_done[t]), ("dW", cl_done["idb"])]
                    ws += bank_waits([bank])
                tt = emit("tensor", (lambda t, bank, j, hb: lambda e: e.transpose(
                    out=psb_bf(bank)[:, j * 128:(j + 1) * 128],
                    in_=xnt_t(t)[:, hb * 128:(hb + 1) * 128],
                    identity=idb[:]))(t, bank, j, hb),
                    waits=ws, inc="sT" if j == 7 else None)
                if j == 7:
                    tp0_done[t][k2] = tt

    def p0_evac(t):
        for k2 in range(2):
            bank = (2 * t + k2) % 4 if t < 4 else 4 + (2 * t + k2) % 4
            ev = emit("scalar", (lambda bank, k2, t: lambda e: e.activation(
                out=xnTr[:, k2 * 8:(k2 + 1) * 8, t * 128:(t + 1) * 128],
                in_=psb_bf(bank).rearrange("p (j f) -> p j f", f=128),
                func=AF.Copy))(bank, k2, t),
                waits=[("sT", tp0_done[t][k2])], inc="sS")
            evac_done[t][k2] = ev
            bank_free[bank] = ("sS", ev)

    sqv_arr = [None] * NT
    p0_dma(0)
    p0_dma(1)
    p0_sq(0)
    p0_red(0)
    p0_sq(1)
    p0_red(1)
    sqv_arr[0] = p0_sqrt(0, p0_chain(0))
    for t in range(NT):
        if t + 2 < NT:
            p0_dma(t + 2)
        if 1 <= t <= 4:
            p0_evac(t - 1)
        if t + 1 < NT:
            sqv_arr[t + 1] = p0_sqrt(t + 1, p0_chain(t + 1))
        p0_norm(t, sqv_arr[t])
        if t == 1:
            p1_dma(0)
            p1_dma(1)
        if t == 2:
            p1_deqg(0)
            p1_deqg(1)
        if t == 3:
            p1_dequ(0)
            p1_dequ(1)
        if t + 2 < NT:
            p0_sq(t + 2)
            p0_red(t + 2)
        if t <= 3:
            p0_transp(t)
    emit("scalar", lambda e: e.preload_act_table(AF.Silu))

    xnT_ready = evac_done[3][1]

    for half in range(2):
        base = half * NI
        for i in range(NI):
            it = base + i
            if i + 2 < NI:
                p1_dma(it + 2)
            nxt = it + 1
            if i + 1 < NI and not (nxt <= base + 1):
                p1_deqg(nxt)
                p1_dequ(nxt)
            p1_mm(it, half)
            if i == NI - 3:
                for pf in range(4):
                    p2_dma(half * 2 * NI + pf)
            if half == 0 and i < 4:
                p0_transp(4 + i)
            if half == 0 and 1 <= i <= 4:
                p0_evac(3 + i)
            p1_evac(it)
        dbase0 = half * 2 * NI
        p2_deq(dbase0)
        p2_deq(dbase0 + 1)
        if half == 0:
            p1_dma(NI)
            p1_dma(NI + 1)
            p1_deqg(NI)
            p1_dequ(NI, eng="scalar")
            p1_deqg(NI + 1)
            p1_dequ(NI + 1, eng="scalar")
        for w in range(2):
            wit = half * 2 + w
            dbase = wit * NI
            for i in range(NI):
                dit = dbase + i
                if dit >= dbase0 + 2:
                    p2_deq(dit)
                if dit + 4 < dbase0 + 2 * NI:
                    p2_dma(dit + 4)
                p2_mm(dit, half, i)
                if half == 1 and w == 1:
                    if i == 36:
                        emit("scalar", lambda e: e.preload_act_table(AF.Exp))
                        p3_exp(0)
                    elif i == 38:
                        p3_exp(1)
                    elif i == 40:
                        p3_wbuild(0)
                    elif i == 42:
                        p3_exp(2)
                    elif i == 44:
                        p3_wbuild(1)
                    elif i == 46:
                        p3_exp(3)
                    elif i == 48:
                        p3_wbuild(2)
                    elif i == 50:
                        p3_exp(4)
                    elif i == 52:
                        p3_wbuild(3)
                    elif i == 54:
                        p3_exp(5)
                    elif i == 56:
                        p3_wbuild(4)
                    elif i == 58:
                        p3_exp(6)
                    elif i == 60:
                        p3_wbuild(5)
                    elif i == 62:
                        p3_wbuild(6)
            if half == 1 and w == 1:
                emit("scalar", lambda e: e.preload_act_table(AF.Silu))
            p2_evac(half, w, wit)
            if half == 1 and w == 1:
                emit("scalar", lambda e: e.preload_act_table(AF.Exp))

    def p3_xsl(b):
        pb = b % 2
        if b < 2:
            ws = [h_freed(range(46 + pb * 4, 50 + pb * 4))]
        else:
            ws = [("sV", add_done[b - 2])]
        xsl_ld[b] = emit("sync", (lambda b, pb: lambda e: e.dma_start(
            out=xslv(pb).rearrange("p (j f) -> p j f", f=128),
            in_=x3[:, :, b * 128:(b + 1) * 128]))(b, pb), waits=ws, inc="dW", amt=16)

    def p3_attn(b):
        sbi = b % 7
        pb = b % 2
        for th in range(2):
            pa = pb * 2 + th
            for a in range(b + 1):
                d = b - a
                ws = []
                if a == 0:
                    ws = [("sV", wd_done[b])] + bank_waits([pa])
                tt = emit("tensor", (lambda sbi, pa, a, d, th: lambda e: e.matmul(
                    out=psb(pa),
                    lhsT=strip0(sbi)[:, d * 128:(d + 1) * 128],
                    rhs=xnT[:, a * TOK + th * TH: a * TOK + th * TH + TH],
                    start=(a == 0), stop=(a == b)))(sbi, pa, a, d, th),
                    waits=ws, inc="sT" if a == b else None)
            attn_done[b][th] = tt

    def p3_pmul(b):
        pb = b % 2
        ws = [("sT", attn_done[b][1]),
              ("sS", evD[(1, b // 8, b % 8)])]
        if b >= 2:
            ws.append(("sT", tp_done[b - 2]))
        pmul_done[b] = emit("vector", (lambda pb, b: lambda e: e.scalar_tensor_tensor(
            out=pTv(pb).rearrange("p (t f) -> p t f", f=512),
            in0=psap[:, pb * 2:pb * 2 + 2, :],
            scalar=biasr[:, b:b + 1],
            in1=gout[:, b * TOK:(b + 1) * TOK].rearrange("p (t f) -> p t f", f=512),
            op0=ALU.add, op1=ALU.mult))(pb, b), waits=ws, inc="sV")
        bank_free[pb * 2] = ("sV", pmul_done[b])
        bank_free[pb * 2 + 1] = ("sV", pmul_done[b])

    def p3_transp(b):
        pb = b % 2
        for j in range(8):
            bank = 4 + 2 * pb + j // 4
            ws = []
            if j == 0:
                ws = [("sV", pmul_done[b])] + bank_waits([4 + 2 * pb, 5 + 2 * pb])
            tt = emit("tensor", (lambda pb, j, bank: lambda e: e.transpose(
                out=psap[:, bank, (j % 4) * 128:(j % 4) * 128 + 128],
                in_=pTv(pb)[:, j * 128:(j + 1) * 128],
                identity=idf[:]))(pb, j, bank),
                waits=ws, inc="sT" if j == 7 else None)
        tp_done[b] = tt

    def p3_add(b):
        pb = b % 2
        ws = [("sT", tp_done[b]), ("dW", xsl_ld[b])]
        if b < 2:
            ws.append(h_freed(range(54 + pb * 4, 58 + pb * 4)))
        if b >= 2:
            ws.append(("sO", out_st[b - 2]))
        add_done[b] = emit("vector", (lambda pb: lambda e: e.tensor_tensor(
            out=obufv(pb).rearrange("p (t f) -> p t f", f=512),
            in0=psap[:, 4 + 2 * pb:6 + 2 * pb, :],
            in1=xslv(pb).rearrange("p (t f) -> p t f", f=512),
            op=ALU.add))(pb), waits=ws, inc="sV")
        bank_free[4 + 2 * pb] = ("sV", add_done[b])
        bank_free[5 + 2 * pb] = ("sV", add_done[b])

    def p3_store(b):
        pb = b % 2
        out_st[b] = emit("gpsimd", (lambda pb, b: lambda e: e.dma_start(
            out=y3[:, :, b * 128:(b + 1) * 128],
            in_=obufv(pb).rearrange("p (j f) -> p j f", f=128)))(pb, b),
            waits=[("sV", add_done[b])], inc="sO", amt=16)

    for b in range(HB):
        p3_attn(b)
        if b + 7 < HB:
            p3_exp(b + 7)
        if b >= 2:
            p3_transp(b - 2)
        p3_pmul(b)
        if b >= 2:
            p3_add(b - 2)
        p3_xsl(b)
        if b + 7 < HB:
            p3_wbuild(b + 7)
        if b >= 2:
            p3_store(b - 2)
    for b in (HB - 2, HB - 1):
        p3_transp(b)
        p3_add(b)
        p3_store(b)
    emit("gpsimd", lambda e: e.nop(), waits=[("sO", semv["sO"])])

    sems = {}
    for name in ("dW", "dX", "sT", "sS", "sV", "sO"):
        sems[name] = ctx.enter_context(nc.semaphore(name))
    block = ctx.enter_context(nc.Block())

    def runner(engname):
        def _run(eng):
            for ws, fn, inc, amt in steps[engname]:
                for (s, vv) in ws:
                    eng.wait_ge(sems[s], vv)
                ins = fn(eng)
                if inc is not None:
                    ins.then_inc(sems[inc], amt)
        return _run

    block.sync(runner("sync"))
    block.scalar(runner("scalar"))
    block.vector(runner("vector"))
    block.tensor(runner("tensor"))
    block.gpsimd(runner("gpsimd"))
    ctx.close()
    return nc


def _low_byte(w):
    w = np.ascontiguousarray(np.asarray(w, dtype=np.int32))
    assert np.little_endian
    return w.view(np.uint8).reshape(w.shape[0], w.shape[1], 4)[:, :, 0]


def kernel(L0_attn_x, L0_attn_weight, L0_attn_bias, L0_attn_decay_value,
           L0_ffn_gate_w, L0_ffn_gate_scale, L0_ffn_up_w, L0_ffn_up_scale,
           L0_ffn_down_w, L0_ffn_down_scale):
    x = np.ascontiguousarray(np.asarray(L0_attn_x, dtype=np.float32).reshape(SEQ, D))
    wgT = np.ascontiguousarray(_low_byte(L0_ffn_gate_w).T)
    wuT = np.ascontiguousarray(_low_byte(L0_ffn_up_w).T)
    wdT = np.ascontiguousarray(_low_byte(L0_ffn_down_w).T)
    gsc = np.ascontiguousarray(np.asarray(L0_ffn_gate_scale, np.float32).reshape(NI, 128).T)
    usc = np.ascontiguousarray(np.asarray(L0_ffn_up_scale, np.float32).reshape(NI, 128).T)
    dsc = np.ascontiguousarray(np.asarray(L0_ffn_down_scale, np.float32).reshape(HB, 128).T)
    biasr = np.ascontiguousarray(np.asarray(L0_attn_bias, np.float32).reshape(HB, 128).T)
    wgt = np.asarray(L0_attn_weight, np.float32)
    w0r = np.ascontiguousarray(wgt[0].reshape(HB, 128).T)
    w0rev = np.ascontiguousarray(w0r[:, ::-1])
    w1row = np.ascontiguousarray(wgt[1].reshape(1, D))
    dvrow = np.ascontiguousarray(np.asarray(L0_attn_decay_value, np.float32).reshape(1, 2))
    eall = (np.arange(D)[None, :] - np.arange(128)[:, None]).astype(np.float16)
    mtile = (np.arange(128)[None, :] >= np.arange(128)[:, None]).astype(ml_dtypes.bfloat16)
    idbf = np.eye(128, dtype=ml_dtypes.bfloat16)
    idf32 = np.eye(128, dtype=np.float32)

    shared = {"wgT": wgT, "wuT": wuT, "wdT": wdT, "gsc": gsc, "usc": usc,
              "dsc": dsc, "biasr": biasr, "w0r": w0r, "w0rev": w0rev, "w1row": w1row,
              "dvrow": dvrow, "eall": eall, "mtile": mtile, "idbf": idbf,
              "idf32": idf32}
    in_maps = []
    for c in range(NCORE):
        m = dict(shared)
        m["x"] = np.ascontiguousarray(x[c * TOK:(c + 1) * TOK])
        in_maps.append(m)

    nc = build_program()
    res = run_bass_kernel_spmd(nc, in_maps, core_ids=list(range(NCORE)))
    y = np.concatenate([res.results[c]["y"] for c in range(NCORE)], axis=0)
    return y.reshape(1, SEQ, D)


# version 16
# speedup vs baseline: 1.1952x; 1.0044x over previous
import numpy as np
import ml_dtypes
from contextlib import ExitStack

import concourse.bass as bass
import concourse.mybir as mybir
from concourse.bass_utils import run_bass_kernel_spmd

F32 = mybir.dt.float32
F16 = mybir.dt.float16
BF16 = mybir.dt.bfloat16
U8 = mybir.dt.uint8
AF = mybir.ActivationFunctionType
ALU = mybir.AluOpType
AX = mybir.AxisListType

D = 2048
SEQ = 8192
NCORE = 8
TOK = SEQ // NCORE
NT = TOK // 128
HB = D // 128
NI = 64
TH = 512


def build_program():
    nc = bass.Bass("TRN2", target_bir_lowering=False)

    x_d = nc.dram_tensor("x", [TOK, D], F32, kind="ExternalInput")
    wgT_d = nc.dram_tensor("wgT", [D, 8192], U8, kind="ExternalInput")
    wuT_d = nc.dram_tensor("wuT", [D, 8192], U8, kind="ExternalInput")
    wdT_d = nc.dram_tensor("wdT", [8192, D], U8, kind="ExternalInput")
    gsc_d = nc.dram_tensor("gsc", [128, NI], F32, kind="ExternalInput")
    usc_d = nc.dram_tensor("usc", [128, NI], F32, kind="ExternalInput")
    dsc_d = nc.dram_tensor("dsc", [128, HB], F32, kind="ExternalInput")
    bias_d = nc.dram_tensor("biasr", [128, HB], F32, kind="ExternalInput")
    w0_d = nc.dram_tensor("w0r", [128, HB], F32, kind="ExternalInput")
    w0rev_d = nc.dram_tensor("w0rev", [128, HB], F32, kind="ExternalInput")
    w1_d = nc.dram_tensor("w1row", [1, D], F32, kind="ExternalInput")
    dv_d = nc.dram_tensor("dvrow", [1, 2], F32, kind="ExternalInput")
    eall_d = nc.dram_tensor("eall", [128, D], F16, kind="ExternalInput")
    mt_d = nc.dram_tensor("mtile", [128, 128], BF16, kind="ExternalInput")
    idb_d = nc.dram_tensor("idbf", [128, 128], BF16, kind="ExternalInput")
    idf_d = nc.dram_tensor("idf32", [128, 128], F32, kind="ExternalInput")
    lnscr_d = nc.dram_tensor("lnscr", [1, 2], F32, kind="Internal")
    y_d = nc.dram_tensor("y", [TOK, D], F32, kind="ExternalOutput")

    x3 = x_d.ap().rearrange("(j p) f -> p j f", p=128)
    y3 = y_d.ap().rearrange("(j p) f -> p j f", p=128)
    wgT3 = wgT_d.ap().rearrange("(hb p) i -> p hb i", p=128)
    wuT3 = wuT_d.ap().rearrange("(hb p) i -> p hb i", p=128)
    wdT3 = wdT_d.ap().rearrange("(ib p) h -> p ib h", p=128)

    ctx = ExitStack()

    def sb(name, shape, dt):
        return ctx.enter_context(nc.sbuf_tensor(name, shape, dt))

    xnT = sb("xnT", [128, HB * TOK], BF16)
    gout = sb("gout", [128, HB * TOK], F32)
    arena = sb("arena", [128, 65536], U8)
    wgu = sb("wgu", [128, 2, D], U8)
    wgb = sb("wgb", [128, 2, D], BF16)
    wuu = sb("wuu", [128, 2, D], U8)
    wub = sb("wub", [128, 2, D], BF16)
    wdu = sb("wdu", [128, 4, 1024], U8)
    wdb = sb("wdb", [128, 2, 1024], BF16)
    at_t = sb("at", [128, 2, TH], BF16)
    bt_t = sb("bt", [128, 2, TH], BF16)
    eall = sb("ealls", [128, D], F16)
    w1b = sb("w1b", [128, D], BF16)
    mt = sb("mt", [128, 128], BF16)
    idb = sb("idb", [128, 128], BF16)
    idf = sb("idf", [128, 128], F32)
    gsc = sb("gscs", [128, NI], F32)
    usc = sb("uscs", [128, NI], F32)
    dsc = sb("dscs", [128, HB], F32)
    biasr = sb("biasrs", [128, HB], F32)
    w0r = sb("w0rs", [128, HB], F32)
    w0rev = sb("w0revs", [128, HB], F32)
    dvs = sb("dvs", [1, 2], F32)
    dvc = sb("dvc", [1, 2], F32)
    lndv = sb("lndv", [1, 2], F32)
    lnrep = sb("lnrep", [128, 2], F32)
    stats = sb("stats", [128, 2, 8], F32)

    PS = ctx.enter_context(nc.psum_tensor("PS", [128, 8, 512], F32))

    A = arena.ap()

    def xbuf(t):
        b = t % 4
        return A[:, b * 8192:(b + 1) * 8192].bitcast(F32)

    def xnt_t(t):
        b = t % 4
        return A[:, 32768 + b * 4096:32768 + (b + 1) * 4096].bitcast(BF16)

    scr = A[:, 49152:57344].bitcast(F32)

    hp_all = A[:, 0:65536].bitcast(BF16)

    def hp(i):
        return hp_all[:, i * TH:(i + 1) * TH]

    def strip0(sbi):
        return A[:, 8192 + sbi * 4096:8192 + (sbi + 1) * 4096].bitcast(BF16)

    def strip1(sbi):
        return A[:, sbi * 4096:(sbi + 1) * 4096].bitcast(BF16)

    def pTv(pb):
        return A[:, 38912 + pb * 4096:38912 + (pb + 1) * 4096].bitcast(F32)

    def xslv(pb):
        return A[:, 47104 + pb * 4096:47104 + (pb + 1) * 4096].bitcast(F32)

    def obufv(pb):
        return A[:, 55296 + pb * 4096:55296 + (pb + 1) * 4096].bitcast(F32)

    psap = PS.ap()

    def psb(k):
        return psap[:, k, :]

    def psb_bf(k):
        return psap[:, k, :].bitcast(BF16)

    ENGS = ("sync", "scalar", "vector", "tensor", "gpsimd")
    steps = {e: [] for e in ENGS}
    semv = {s: 0 for s in ("dW", "dX", "sT", "sS", "sV", "sO")}

    def emit(eng, fn, waits=(), inc=None, amt=1):
        for (s, v) in waits:
            assert v is not None, "wait references an un-emitted milestone"
        ws = tuple((s, v) for (s, v) in waits if v > 0)
        steps[eng].append((ws, fn, inc, amt))
        if inc is not None:
            semv[inc] += amt
        return semv[inc] if inc is not None else 0

    bank_free = [None] * 8

    def bank_waits(banks):
        ws = []
        for k in banks:
            if bank_free[k] is not None and bank_free[k] not in ws:
                ws.append(bank_free[k])
        return ws

    cl = [(dvs, dv_d), (idb, idb_d), (gsc, gsc_d), (usc, usc_d), (dsc, dsc_d),
          (biasr, bias_d), (w0r, w0_d), (w0rev, w0rev_d), (idf, idf_d), (mt, mt_d), (eall, eall_d)]
    cl_done = {}
    for dst, src in cl:
        dW_consts = emit(
            "sync",
            (lambda dst, src: lambda e: e.dma_start(out=dst[:], in_=src.ap()))(dst, src),
            inc="dW", amt=16)
        cl_done[dst.name] = dW_consts

    v = emit("vector", lambda e: e.tensor_scalar(
        out=dvc[:], in0=dvs[:], scalar1=0.9, scalar2=None, op0=ALU.max),
        waits=[("dW", cl_done["dvs"])], inc="sV")
    v = emit("vector", lambda e: e.tensor_scalar(
        out=dvc[:], in0=dvc[:], scalar1=1.0, scalar2=None, op0=ALU.min),
        waits=[("sV", v)], inc="sV")
    emit("scalar", lambda e: e.preload_act_table(AF.Ln))
    sS_ln = emit("scalar", lambda e: e.activation(out=lndv[:], in_=dvc[:], func=AF.Ln),
                 waits=[("sV", v)], inc="sS")
    emit("scalar", lambda e: e.preload_act_table(AF.Sqrt))
    dw = emit("sync", lambda e: e.dma_start(out=lnscr_d.ap(), in_=lndv[:]),
              waits=[("sS", sS_ln)], inc="dW", amt=16)
    dW_lnrep = emit("sync", lambda e: e.dma_start(
        out=lnrep[:], in_=lnscr_d.ap().broadcast_to((128, 2))),
        waits=[("dW", dw)], inc="dW", amt=16)

    emit("gpsimd", lambda e: e.dma_start(
        out=w1b[:], in_=w1_d.ap().broadcast_to((128, D))), inc="sO", amt=16)

    NIT = 2 * NI
    wgld = [None] * NIT
    wuld = [None] * NIT
    deqg = [None] * NIT
    dequ = [None] * NIT
    mmg = [None] * NIT
    mmu = [None] * NIT
    evA = [None] * NIT
    evB = [None] * NIT
    hmul = [None] * NIT
    NDIT = 4 * NI
    wdld = [None] * NDIT
    deqd = [None] * NDIT
    mmd = [None] * NDIT
    evD = {}

    def p1_dma(it):
        i = it % NI
        ws = [deqg[it - 2]] if it >= 2 else []
        wgld[it] = emit("sync", (lambda it, i: lambda e: e.dma_start(
            out=wgu[:, it % 2, :], in_=wgT3[:, :, i * 128:(i + 1) * 128]))(it, i),
            waits=ws, inc="dW", amt=16)
        ws = [dequ[it - 2]] if it >= 2 else []
        wuld[it] = emit("sync", (lambda it, i: lambda e: e.dma_start(
            out=wuu[:, it % 2, :], in_=wuT3[:, :, i * 128:(i + 1) * 128]))(it, i),
            waits=ws, inc="dW", amt=16)

    def p1_deqg(it, eng="scalar"):
        ws = [("dW", wgld[it])]
        if it >= 2:
            ws.append(("sT", mmg[it - 2]))
        if eng == "gpsimd":
            val = emit(eng, (lambda it: lambda e: e.tensor_scalar(
                out=wgb[:, it % 2, :], in0=wgu[:, it % 2, :], scalar1=-128.0,
                scalar2=None, op0=ALU.add))(it), waits=ws, inc="sO")
            deqg[it] = ("sO", val)
            return
        val = emit(eng, (lambda it: lambda e: e.activation(
            out=wgb[:, it % 2, :], in_=wgu[:, it % 2, :], func=AF.Copy,
            bias=-128.0))(it), waits=ws, inc="sS")
        deqg[it] = ("sS", val)

    def p1_dequ(it, eng="vector"):
        ws = [("dW", wuld[it])]
        if it >= 2:
            ws.append(("sT", mmu[it - 2]))
        if eng == "gpsimd":
            val = emit(eng, (lambda it: lambda e: e.tensor_scalar(
                out=wub[:, it % 2, :], in0=wuu[:, it % 2, :], scalar1=-128.0,
                scalar2=None, op0=ALU.add))(it), waits=ws, inc="sO")
            dequ[it] = ("sO", val)
            return
        if eng == "vector":
            val = emit(eng, (lambda it: lambda e: e.tensor_scalar(
                out=wub[:, it % 2, :], in0=wuu[:, it % 2, :], scalar1=-128.0,
                scalar2=None, op0=ALU.add))(it), waits=ws, inc="sV")
            dequ[it] = ("sV", val)
        else:
            val = emit(eng, (lambda it: lambda e: e.activation(
                out=wub[:, it % 2, :], in_=wuu[:, it % 2, :], func=AF.Copy,
                bias=-128.0))(it), waits=ws, inc="sS")
            dequ[it] = ("sS", val)

    def p1_mm(it, half):
        p = it % 2
        for which in range(2):
            bankk = p if which == 0 else 2 + p
            wb = wgb if which == 0 else wub
            dq = deqg[it] if which == 0 else dequ[it]
            for hb in range(HB):
                ws = []
                if hb == 0:
                    ws = [dq] + bank_waits([bankk])
                    if it == 0 and which == 0:
                        ws.append(("sS", xnT_ready))
                    if it == NI and which == 0:
                        ws.append(("sS", evac_done[NT - 1][1]))
                tt = emit("tensor", (lambda p, wb, hb, bankk, half: lambda e: e.matmul(
                    out=psb(bankk),
                    lhsT=wb[:, p, hb * 128:(hb + 1) * 128],
                    rhs=xnT[:, hb * TOK + half * TH: hb * TOK + half * TH + TH],
                    start=(hb == 0), stop=(hb == HB - 1)))(p, wb, hb, bankk, half),
                    waits=ws, inc="sT" if hb == HB - 1 else None)
            if which == 0:
                mmg[it] = tt
            else:
                mmu[it] = tt

    def p1_evac(it):
        p = it % 2
        ab = it % 2
        i = it % NI
        ws = [("sT", mmg[it])]
        if it >= 2:
            ws.append(("sV", hmul[it - 2]))
        evA[it] = emit("scalar", (lambda p, ab, i: lambda e: e.activation(
            out=at_t[:, ab, :], in_=psb(p), func=AF.Silu,
            scale=gsc[:, i:i + 1]))(p, ab, i), waits=ws, inc="sS")
        bank_free[p] = ("sS", evA[it])
        evB[it] = emit("vector", (lambda p, ab: lambda e: e.tensor_copy(
            out=bt_t[:, ab, :], in_=psb(2 + p)))(p, ab),
            waits=[("sT", mmu[it])], inc="sV")
        bank_free[2 + p] = ("sV", evB[it])
        ws = [("sS", evA[it]), ("sV", evB[it])]
        if it >= NI:
            ws.append(("sT", mmd[NI + i]))
        else:
            if i < 32:
                s = i // 8
                ws += [("sV", norm_done[s + 4]), ("sS", sq_done[s + 4])]
            elif i < 48:
                b2 = (i - 32) // 4
                ws.append(("sT", tp0_done[b2 + 4][1]))
            elif i < 56:
                ws.append(("sS", sq_done[7]))
        hmul[it] = emit("vector", (lambda ab, i: lambda e: e.tensor_tensor(
            out=hp(i), in0=at_t[:, ab, :], in1=bt_t[:, ab, :],
            op=ALU.mult))(ab, i), waits=ws, inc="sV")

    def p2_dma(dit):
        w = (dit // NI) % 2
        i = dit % NI
        ws = [("sV", deqd[dit - 4])] if dit >= 4 else []
        wdld[dit] = emit("sync", (lambda dit, i, w: lambda e: e.dma_start(
            out=wdu[:, dit % 4, :], in_=wdT3[:, i, w * 1024:(w + 1) * 1024]))(dit, i, w),
            waits=ws, inc="dW", amt=16)

    def p2_deq(dit):
        i = dit % NI
        ws = [("dW", wdld[dit])]
        if dit >= 2:
            ws.append(("sT", mmd[dit - 2]))
        deqd[dit] = emit("vector", (lambda dit, i: lambda e: e.tensor_scalar(
            out=wdb[:, dit % 2, :], in0=wdu[:, dit % 4, :], scalar1=-128.0,
            scalar2=usc[:, i:i + 1], op0=ALU.add, op1=ALU.mult))(dit, i),
            waits=ws, inc="sV")

    def p2_mm(dit, half, i):
        q = dit % 2
        for k in range(8):
            ws = []
            if k == 0:
                ws = [("sV", deqd[dit]), ("sV", hmul[half * NI + i])]
            if i == 0:
                ws += bank_waits([k])
            tt = emit("tensor", (lambda q, k, i: lambda e: e.matmul(
                out=psb(k), lhsT=wdb[:, q, k * 128:(k + 1) * 128],
                rhs=hp(i), start=(i == 0), stop=(i == NI - 1)))(q, k, i),
                waits=ws, inc="sT" if k == 7 else None)
        mmd[dit] = tt

    exp_done = [None] * HB
    wseg_done = [None] * HB
    wd_done = [None] * HB
    attn_done = [[None, None] for _ in range(HB)]
    pmul_done = [None] * HB
    tp_done = [None] * HB
    add_done = [None] * HB
    xsl_ld = [None] * HB
    out_st = [None] * HB

    def h_freed(blocks):
        return ("sT", mmd[3 * NI + max(blocks)])

    def p3_exp(b):
        sbi = b % 7
        s1i = b % 2
        n = (b + 1) * 128
        ws = [("dW", dW_lnrep)]
        if b < 7:
            ws.append(h_freed(range(8 + 4 * sbi, 12 + 4 * sbi)))
        else:
            ws.append(("sT", attn_done[b - 7][1]))
        emit("scalar", (lambda sbi, n: lambda e: e.activation(
            out=strip0(sbi)[:, 0:n], in_=eall[:, 0:n], func=AF.Exp,
            scale=lnrep[:, 0:1]))(sbi, n), waits=ws, inc="sS")
        ws = []
        if b < 2:
            ws.append(h_freed(range(4 * s1i, 4 + 4 * s1i)))
        else:
            ws.append(("sV", wseg_done[b - 2]))
        exp_done[b] = emit("scalar", (lambda s1i, n: lambda e: e.activation(
            out=strip1(s1i)[:, 0:n], in_=eall[:, 0:n], func=AF.Exp,
            scale=lnrep[:, 1:2]))(s1i, n), waits=ws, inc="sS")

    def p3_wbuild(b):
        sbi = b % 7
        nseg = b + 1
        s0_3 = strip0(sbi)[:, 0:nseg * 128].rearrange("p (s f) -> p s f", f=128)
        s1_3 = strip1(b % 2)[:, 0:nseg * 128].rearrange("p (s f) -> p s f", f=128)
        w0bc = w0rev.ap()[:, HB - 1 - b:HB].unsqueeze(2).broadcast_to((128, nseg, 128))
        w1bc = w1b.ap()[:, b * 128:(b + 1) * 128].unsqueeze(1).broadcast_to((128, nseg, 128))
        v1 = emit("vector", (lambda s0_3, w0bc: lambda e: e.tensor_tensor(
            out=s0_3, in0=s0_3, in1=w0bc, op=ALU.mult))(s0_3, w0bc),
            waits=[("sS", exp_done[b]), ("sO", 16)], inc="sV")
        v2 = emit("vector", (lambda s1_3, w1bc: lambda e: e.tensor_tensor(
            out=s1_3, in0=s1_3, in1=w1bc, op=ALU.mult))(s1_3, w1bc),
            inc="sV")
        wseg_done[b] = emit("vector", (lambda s0_3, s1_3: lambda e: e.tensor_tensor(
            out=s0_3, in0=s0_3, in1=s1_3, op=ALU.add))(s0_3, s1_3),
            waits=[("sV", v2)], inc="sV")
        wd_done[b] = emit("vector", (lambda sbi: lambda e: e.tensor_tensor(
            out=strip0(sbi)[:, 0:128], in0=strip0(sbi)[:, 0:128], in1=mt[:],
            op=ALU.mult))(sbi), waits=[("sV", wseg_done[b])], inc="sV")

    def p2_evac(half, w, wit):
        for k in range(8):
            hh = w * 8 + k
            ev = emit("scalar", (lambda k, hh, half: lambda e: e.activation(
                out=gout[:, hh * TOK + half * TH: hh * TOK + half * TH + TH],
                in_=psb(k), func=AF.Silu, scale=dsc[:, hh:hh + 1]))(k, hh, half),
                waits=[("sT", mmd[wit * NI + NI - 1])], inc="sS")
            evD[(half, w, k)] = ev
            bank_free[k] = ("sS", ev)


    norm_done = [None] * NT
    red_done = [None] * NT
    sq_done = [None] * NT
    var_done = [None] * NT
    xld = [None] * NT
    tp0_done = [[None, None] for _ in range(NT)]
    evac_done = [[None, None] for _ in range(NT)]
    xnTr = xnT.ap().rearrange("p (hb tok) -> p hb tok", tok=TOK)

    def p0_dma(t):
        ws = []
        if t >= 4:
            ws = [("sV", norm_done[t - 4])]
        xld[t] = emit("scalar", (lambda t: lambda e: e.dma_start(
            out=xbuf(t), in_=x3[:, t, :]))(t), waits=ws, inc="dX", amt=16)

    def p0_sq(t):
        b = t % 2
        ws = [("dX", xld[t])]
        if t >= 2:
            ws.append(("sV", var_done[t - 2]))
        sq_done[t] = emit("scalar", (lambda t, b: lambda e: e.activation(
            out=scr, in_=xbuf(t), func=AF.Square,
            accum_out=stats[:, b, 0:1]))(t, b), waits=ws, inc="sS")

    def p0_red(t):
        b = t % 2
        ws = [("dX", xld[t])]
        if t >= 2:
            ws.append(("sV", var_done[t - 2]))
        red_done[t] = emit("vector", (lambda t, b: lambda e: e.reduce_sum(
            out=stats[:, b, 1:2], in_=xbuf(t), axis=AX.X))(t, b),
            waits=ws, inc="sV")

    def p0_chain(t):
        b = t % 2
        ws = [("sV", red_done[t])]
        if t >= 2:
            ws.append(("sS", sqv_arr[t - 2]))
        v = emit("vector", (lambda b: lambda e: e.tensor_scalar(
            out=stats[:, b, 2:3], in0=stats[:, b, 1:2], scalar1=-1.0 / D,
            scalar2=None, op0=ALU.mult))(b),
            waits=ws, inc="sV")
        v = emit("vector", (lambda b: lambda e: e.tensor_tensor(
            out=stats[:, b, 3:4], in0=stats[:, b, 2:3], in1=stats[:, b, 2:3],
            op=ALU.mult))(b), waits=[("sV", v)], inc="sV")
        var_done[t] = emit("vector", (lambda b: lambda e: e.scalar_tensor_tensor(
            out=stats[:, b, 4:5], in0=stats[:, b, 0:1], scalar=1.0 / D,
            in1=stats[:, b, 3:4], op0=ALU.mult, op1=ALU.subtract))(b),
            waits=[("sV", v), ("sS", sq_done[t])], inc="sV")
        veps = emit("vector", (lambda b: lambda e: e.tensor_scalar(
            out=stats[:, b, 3:4], in0=stats[:, b, 4:5], scalar1=1e-5,
            scalar2=None, op0=ALU.add))(b), waits=[("sV", var_done[t])], inc="sV")
        return veps

    def p0_sqrt(t, veps):
        b = t % 2
        return emit("scalar", (lambda b: lambda e: e.activation(
            out=stats[:, b, 5:6], in_=stats[:, b, 3:4], func=AF.Sqrt))(b),
            waits=[("sV", veps)], inc="sS")

    def p0_norm(t, sqv):
        b = t % 2
        rcp = emit("vector", (lambda b: lambda e: e.reciprocal(
            out=stats[:, b, 6:7], in_=stats[:, b, 5:6]))(b),
            waits=[("sS", sqv)], inc="sV")
        ws = [("sV", rcp)]
        if t >= 4:
            ws.append(("sT", tp0_done[t - 4][1]))
        norm_done[t] = emit("vector", (lambda t, b: lambda e: e.tensor_scalar(
            out=xnt_t(t), in0=xbuf(t), scalar1=stats[:, b, 2:3],
            scalar2=stats[:, b, 6:7], op0=ALU.add, op1=ALU.mult))(t, b),
            waits=ws, inc="sV")

    def p0_transp(t):
        for k2 in range(2):
            bank = (2 * t + k2) % 4 if t < 4 else 4 + (2 * t + k2) % 4
            for j in range(8):
                hb = k2 * 8 + j
                ws = []
                if j == 0:
                    ws = [("sV", norm_done[t]), ("dW", cl_done["idb"])]
                    ws += bank_waits([bank])
                tt = emit("tensor", (lambda t, bank, j, hb: lambda e: e.transpose(
                    out=psb_bf(bank)[:, j * 128:(j + 1) * 128],
                    in_=xnt_t(t)[:, hb * 128:(hb + 1) * 128],
                    identity=idb[:]))(t, bank, j, hb),
                    waits=ws, inc="sT" if j == 7 else None)
                if j == 7:
                    tp0_done[t][k2] = tt

    def p0_evac(t):
        for k2 in range(2):
            bank = (2 * t + k2) % 4 if t < 4 else 4 + (2 * t + k2) % 4
            ev = emit("scalar", (lambda bank, k2, t: lambda e: e.activation(
                out=xnTr[:, k2 * 8:(k2 + 1) * 8, t * 128:(t + 1) * 128],
                in_=psb_bf(bank).rearrange("p (j f) -> p j f", f=128),
                func=AF.Copy))(bank, k2, t),
                waits=[("sT", tp0_done[t][k2])], inc="sS")
            evac_done[t][k2] = ev
            bank_free[bank] = ("sS", ev)

    sqv_arr = [None] * NT
    p0_dma(0)
    p0_dma(1)
    p0_sq(0)
    p0_red(0)
    p0_sq(1)
    p0_red(1)
    sqv_arr[0] = p0_sqrt(0, p0_chain(0))
    for t in range(NT):
        if t + 2 < NT:
            p0_dma(t + 2)
        if 1 <= t <= 4:
            p0_evac(t - 1)
        if t + 1 < NT:
            sqv_arr[t + 1] = p0_sqrt(t + 1, p0_chain(t + 1))
        p0_norm(t, sqv_arr[t])
        if t == 1:
            p1_dma(0)
            p1_dma(1)
        if t == 2:
            p1_deqg(0)
            p1_deqg(1)
        if t == 3:
            p1_dequ(0)
            p1_dequ(1)
        if t + 2 < NT:
            p0_sq(t + 2)
            p0_red(t + 2)
        if t <= 3:
            p0_transp(t)
    emit("scalar", lambda e: e.preload_act_table(AF.Silu))

    xnT_ready = evac_done[3][1]

    for half in range(2):
        base = half * NI
        for i in range(NI):
            it = base + i
            if i + 2 < NI:
                p1_dma(it + 2)
            nxt = it + 1
            if i + 1 < NI and not (nxt <= base + 1):
                p1_deqg(nxt)
                p1_dequ(nxt)
            p1_mm(it, half)
            if i == NI - 3:
                for pf in range(4):
                    p2_dma(half * 2 * NI + pf)
            if i == NI - 2:
                p2_deq(half * 2 * NI)
                p2_deq(half * 2 * NI + 1)
            if half == 0 and i < 4:
                p0_transp(4 + i)
            if half == 0 and 1 <= i <= 4:
                p0_evac(3 + i)
            p1_evac(it)
        dbase0 = half * 2 * NI
        if half == 0:
            p1_dma(NI)
            p1_dma(NI + 1)
            p1_deqg(NI)
            p1_dequ(NI, eng="scalar")
            p1_deqg(NI + 1)
            p1_dequ(NI + 1, eng="scalar")
        for w in range(2):
            wit = half * 2 + w
            dbase = wit * NI
            for i in range(NI):
                dit = dbase + i
                if dit >= dbase0 + 2:
                    p2_deq(dit)
                if dit + 4 < dbase0 + 2 * NI:
                    p2_dma(dit + 4)
                p2_mm(dit, half, i)
                if half == 1 and w == 1:
                    if i == 12:
                        emit("scalar", lambda e: e.preload_act_table(AF.Exp))
                        p3_exp(0)
                    elif i == 15:
                        p3_exp(1)
                    elif i == 18:
                        p3_wbuild(0)
                    elif i == 21:
                        p3_exp(2)
                    elif i == 24:
                        p3_wbuild(1)
                    elif i == 27:
                        p3_exp(3)
                    elif i == 30:
                        p3_wbuild(2)
                    elif i == 33:
                        p3_exp(4)
                    elif i == 36:
                        p3_wbuild(3)
                    elif i == 39:
                        p3_exp(5)
                    elif i == 42:
                        p3_wbuild(4)
                    elif i == 45:
                        p3_exp(6)
                    elif i == 48:
                        p3_wbuild(5)
                    elif i == 52:
                        p3_wbuild(6)
            if half == 1 and w == 1:
                emit("scalar", lambda e: e.preload_act_table(AF.Silu))
            p2_evac(half, w, wit)
            if half == 1 and w == 1:
                emit("scalar", lambda e: e.preload_act_table(AF.Exp))

    def p3_xsl(b):
        pb = b % 2
        if b < 2:
            ws = [h_freed(range(46 + pb * 4, 50 + pb * 4))]
        else:
            ws = [("sV", add_done[b - 2])]
        xsl_ld[b] = emit("sync", (lambda b, pb: lambda e: e.dma_start(
            out=xslv(pb).rearrange("p (j f) -> p j f", f=128),
            in_=x3[:, :, b * 128:(b + 1) * 128]))(b, pb), waits=ws, inc="dW", amt=16)

    def p3_attn(b):
        sbi = b % 7
        pb = b % 2
        for th in range(2):
            pa = pb * 2 + th
            for a in range(b + 1):
                d = b - a
                ws = []
                if a == 0:
                    ws = [("sV", wd_done[b])] + bank_waits([pa])
                tt = emit("tensor", (lambda sbi, pa, a, d, th: lambda e: e.matmul(
                    out=psb(pa),
                    lhsT=strip0(sbi)[:, d * 128:(d + 1) * 128],
                    rhs=xnT[:, a * TOK + th * TH: a * TOK + th * TH + TH],
                    start=(a == 0), stop=(a == b)))(sbi, pa, a, d, th),
                    waits=ws, inc="sT" if a == b else None)
            attn_done[b][th] = tt

    def p3_pmul(b):
        pb = b % 2
        ws = [("sT", attn_done[b][1]),
              ("sS", evD[(1, b // 8, b % 8)])]
        if b >= 2:
            ws.append(("sT", tp_done[b - 2]))
        pmul_done[b] = emit("vector", (lambda pb, b: lambda e: e.scalar_tensor_tensor(
            out=pTv(pb).rearrange("p (t f) -> p t f", f=512),
            in0=psap[:, pb * 2:pb * 2 + 2, :],
            scalar=biasr[:, b:b + 1],
            in1=gout[:, b * TOK:(b + 1) * TOK].rearrange("p (t f) -> p t f", f=512),
            op0=ALU.add, op1=ALU.mult))(pb, b), waits=ws, inc="sV")
        bank_free[pb * 2] = ("sV", pmul_done[b])
        bank_free[pb * 2 + 1] = ("sV", pmul_done[b])

    def p3_transp(b):
        pb = b % 2
        for j in range(8):
            bank = 4 + 2 * pb + j // 4
            ws = []
            if j == 0:
                ws = [("sV", pmul_done[b])] + bank_waits([4 + 2 * pb, 5 + 2 * pb])
            tt = emit("tensor", (lambda pb, j, bank: lambda e: e.transpose(
                out=psap[:, bank, (j % 4) * 128:(j % 4) * 128 + 128],
                in_=pTv(pb)[:, j * 128:(j + 1) * 128],
                identity=idf[:]))(pb, j, bank),
                waits=ws, inc="sT" if j == 7 else None)
        tp_done[b] = tt

    def p3_add(b):
        pb = b % 2
        ws = [("sT", tp_done[b]), ("dW", xsl_ld[b])]
        if b < 2:
            ws.append(h_freed(range(54 + pb * 4, 58 + pb * 4)))
        if b >= 2:
            ws.append(("sO", out_st[b - 2]))
        add_done[b] = emit("vector", (lambda pb: lambda e: e.tensor_tensor(
            out=obufv(pb).rearrange("p (t f) -> p t f", f=512),
            in0=psap[:, 4 + 2 * pb:6 + 2 * pb, :],
            in1=xslv(pb).rearrange("p (t f) -> p t f", f=512),
            op=ALU.add))(pb), waits=ws, inc="sV")
        bank_free[4 + 2 * pb] = ("sV", add_done[b])
        bank_free[5 + 2 * pb] = ("sV", add_done[b])

    def p3_store(b):
        pb = b % 2
        out_st[b] = emit("gpsimd", (lambda pb, b: lambda e: e.dma_start(
            out=y3[:, :, b * 128:(b + 1) * 128],
            in_=obufv(pb).rearrange("p (j f) -> p j f", f=128)))(pb, b),
            waits=[("sV", add_done[b])], inc="sO", amt=16)

    for b in range(HB):
        p3_attn(b)
        if b + 7 < HB:
            p3_exp(b + 7)
        if b >= 2:
            p3_transp(b - 2)
        p3_pmul(b)
        if b >= 2:
            p3_add(b - 2)
        p3_xsl(b)
        if b + 7 < HB:
            p3_wbuild(b + 7)
        if b >= 2:
            p3_store(b - 2)
    for b in (HB - 2, HB - 1):
        p3_transp(b)
        p3_add(b)
        p3_store(b)
    emit("gpsimd", lambda e: e.nop(), waits=[("sO", semv["sO"])])

    sems = {}
    for name in ("dW", "dX", "sT", "sS", "sV", "sO"):
        sems[name] = ctx.enter_context(nc.semaphore(name))
    block = ctx.enter_context(nc.Block())

    def runner(engname):
        def _run(eng):
            for ws, fn, inc, amt in steps[engname]:
                for (s, vv) in ws:
                    eng.wait_ge(sems[s], vv)
                ins = fn(eng)
                if inc is not None:
                    ins.then_inc(sems[inc], amt)
        return _run

    block.sync(runner("sync"))
    block.scalar(runner("scalar"))
    block.vector(runner("vector"))
    block.tensor(runner("tensor"))
    block.gpsimd(runner("gpsimd"))
    ctx.close()
    return nc


def _low_byte(w):
    w = np.ascontiguousarray(np.asarray(w, dtype=np.int32))
    assert np.little_endian
    return w.view(np.uint8).reshape(w.shape[0], w.shape[1], 4)[:, :, 0]


def kernel(L0_attn_x, L0_attn_weight, L0_attn_bias, L0_attn_decay_value,
           L0_ffn_gate_w, L0_ffn_gate_scale, L0_ffn_up_w, L0_ffn_up_scale,
           L0_ffn_down_w, L0_ffn_down_scale):
    x = np.ascontiguousarray(np.asarray(L0_attn_x, dtype=np.float32).reshape(SEQ, D))
    wgT = np.ascontiguousarray(_low_byte(L0_ffn_gate_w).T)
    wuT = np.ascontiguousarray(_low_byte(L0_ffn_up_w).T)
    wdT = np.ascontiguousarray(_low_byte(L0_ffn_down_w).T)
    gsc = np.ascontiguousarray(np.asarray(L0_ffn_gate_scale, np.float32).reshape(NI, 128).T)
    usc = np.ascontiguousarray(np.asarray(L0_ffn_up_scale, np.float32).reshape(NI, 128).T)
    dsc = np.ascontiguousarray(np.asarray(L0_ffn_down_scale, np.float32).reshape(HB, 128).T)
    biasr = np.ascontiguousarray(np.asarray(L0_attn_bias, np.float32).reshape(HB, 128).T)
    wgt = np.asarray(L0_attn_weight, np.float32)
    w0r = np.ascontiguousarray(wgt[0].reshape(HB, 128).T)
    w0rev = np.ascontiguousarray(w0r[:, ::-1])
    w1row = np.ascontiguousarray(wgt[1].reshape(1, D))
    dvrow = np.ascontiguousarray(np.asarray(L0_attn_decay_value, np.float32).reshape(1, 2))
    eall = (np.arange(D)[None, :] - np.arange(128)[:, None]).astype(np.float16)
    mtile = (np.arange(128)[None, :] >= np.arange(128)[:, None]).astype(ml_dtypes.bfloat16)
    idbf = np.eye(128, dtype=ml_dtypes.bfloat16)
    idf32 = np.eye(128, dtype=np.float32)

    shared = {"wgT": wgT, "wuT": wuT, "wdT": wdT, "gsc": gsc, "usc": usc,
              "dsc": dsc, "biasr": biasr, "w0r": w0r, "w0rev": w0rev, "w1row": w1row,
              "dvrow": dvrow, "eall": eall, "mtile": mtile, "idbf": idbf,
              "idf32": idf32}
    in_maps = []
    for c in range(NCORE):
        m = dict(shared)
        m["x"] = np.ascontiguousarray(x[c * TOK:(c + 1) * TOK])
        in_maps.append(m)

    nc = build_program()
    res = run_bass_kernel_spmd(nc, in_maps, core_ids=list(range(NCORE)))
    y = np.concatenate([res.results[c]["y"] for c in range(NCORE)], axis=0)
    return y.reshape(1, SEQ, D)


# version 18
# speedup vs baseline: 1.2025x; 1.0061x over previous
import numpy as np
import ml_dtypes
from contextlib import ExitStack

import concourse.bass as bass
import concourse.mybir as mybir
from concourse.bass_utils import run_bass_kernel_spmd

F32 = mybir.dt.float32
F16 = mybir.dt.float16
BF16 = mybir.dt.bfloat16
U8 = mybir.dt.uint8
AF = mybir.ActivationFunctionType
ALU = mybir.AluOpType
AX = mybir.AxisListType

D = 2048
SEQ = 8192
NCORE = 8
TOK = SEQ // NCORE
NT = TOK // 128
HB = D // 128
NI = 64
TH = 512


def build_program():
    nc = bass.Bass("TRN2", target_bir_lowering=False)

    x_d = nc.dram_tensor("x", [TOK, D], F32, kind="ExternalInput")
    wgT_d = nc.dram_tensor("wgT", [D, 8192], U8, kind="ExternalInput")
    wuT_d = nc.dram_tensor("wuT", [D, 8192], U8, kind="ExternalInput")
    wdT_d = nc.dram_tensor("wdT", [8192, D], U8, kind="ExternalInput")
    gsc_d = nc.dram_tensor("gsc", [128, NI], F32, kind="ExternalInput")
    usc_d = nc.dram_tensor("usc", [128, NI], F32, kind="ExternalInput")
    dsc_d = nc.dram_tensor("dsc", [128, HB], F32, kind="ExternalInput")
    bias_d = nc.dram_tensor("biasr", [128, HB], F32, kind="ExternalInput")
    w0_d = nc.dram_tensor("w0r", [128, HB], F32, kind="ExternalInput")
    w0rev_d = nc.dram_tensor("w0rev", [128, HB], F32, kind="ExternalInput")
    w1_d = nc.dram_tensor("w1row", [1, D], F32, kind="ExternalInput")
    dv_d = nc.dram_tensor("dvrow", [1, 2], F32, kind="ExternalInput")
    eall_d = nc.dram_tensor("eall", [128, D], F16, kind="ExternalInput")
    mt_d = nc.dram_tensor("mtile", [128, 128], BF16, kind="ExternalInput")
    idb_d = nc.dram_tensor("idbf", [128, 128], BF16, kind="ExternalInput")
    idf_d = nc.dram_tensor("idf32", [128, 128], F32, kind="ExternalInput")
    lnscr_d = nc.dram_tensor("lnscr", [1, 2], F32, kind="Internal")
    y_d = nc.dram_tensor("y", [TOK, D], F32, kind="ExternalOutput")

    x3 = x_d.ap().rearrange("(j p) f -> p j f", p=128)
    y3 = y_d.ap().rearrange("(j p) f -> p j f", p=128)
    wgT3 = wgT_d.ap().rearrange("(hb p) i -> p hb i", p=128)
    wuT3 = wuT_d.ap().rearrange("(hb p) i -> p hb i", p=128)
    wdT3 = wdT_d.ap().rearrange("(ib p) h -> p ib h", p=128)

    ctx = ExitStack()

    def sb(name, shape, dt):
        return ctx.enter_context(nc.sbuf_tensor(name, shape, dt))

    xnT = sb("xnT", [128, HB * TOK], BF16)
    gout = sb("gout", [128, HB * TOK], F32)
    arena = sb("arena", [128, 65536], U8)
    wgu = sb("wgu", [128, 2, D], U8)
    wgb = sb("wgb", [128, 2, D], BF16)
    wuu = sb("wuu", [128, 2, D], U8)
    wub = sb("wub", [128, 2, D], BF16)
    wdu = sb("wdu", [128, 4, 1024], U8)
    wdb = sb("wdb", [128, 2, 1024], BF16)
    at_t = sb("at", [128, 2, TH], BF16)
    bt_t = sb("bt", [128, 2, TH], BF16)
    eall = sb("ealls", [128, D], F16)
    w1b = sb("w1b", [128, D], BF16)
    mt = sb("mt", [128, 128], BF16)
    idb = sb("idb", [128, 128], BF16)
    idf = sb("idf", [128, 128], F32)
    gsc = sb("gscs", [128, NI], F32)
    usc = sb("uscs", [128, NI], F32)
    dsc = sb("dscs", [128, HB], F32)
    biasr = sb("biasrs", [128, HB], F32)
    w0r = sb("w0rs", [128, HB], F32)
    w0rev = sb("w0revs", [128, HB], F32)
    dvs = sb("dvs", [1, 2], F32)
    dvc = sb("dvc", [1, 2], F32)
    lndv = sb("lndv", [1, 2], F32)
    lnrep = sb("lnrep", [128, 2], F32)
    stats = sb("stats", [128, 2, 8], F32)

    PS = ctx.enter_context(nc.psum_tensor("PS", [128, 8, 512], F32))

    A = arena.ap()

    def xbuf(t):
        b = t % 4
        return A[:, b * 8192:(b + 1) * 8192].bitcast(F32)

    def xnt_t(t):
        b = t % 4
        return A[:, 32768 + b * 4096:32768 + (b + 1) * 4096].bitcast(BF16)

    scr = A[:, 49152:57344].bitcast(F32)

    hp_all = A[:, 0:65536].bitcast(BF16)

    def hp(i):
        return hp_all[:, i * TH:(i + 1) * TH]

    def strip0(sbi):
        return A[:, 8192 + sbi * 4096:8192 + (sbi + 1) * 4096].bitcast(BF16)

    def strip1(sbi):
        return A[:, sbi * 4096:(sbi + 1) * 4096].bitcast(BF16)

    def pTv(pb):
        return A[:, 38912 + pb * 4096:38912 + (pb + 1) * 4096].bitcast(F32)

    def xslv(pb):
        return A[:, 47104 + pb * 4096:47104 + (pb + 1) * 4096].bitcast(F32)

    def obufv(pb):
        return A[:, 55296 + pb * 4096:55296 + (pb + 1) * 4096].bitcast(F32)

    psap = PS.ap()

    def psb(k):
        return psap[:, k, :]

    def psb_bf(k):
        return psap[:, k, :].bitcast(BF16)

    ENGS = ("sync", "scalar", "vector", "tensor", "gpsimd")
    steps = {e: [] for e in ENGS}
    semv = {s: 0 for s in ("dW", "dX", "sT", "sS", "sV", "sO")}

    def emit(eng, fn, waits=(), inc=None, amt=1):
        for (s, v) in waits:
            assert v is not None, "wait references an un-emitted milestone"
        ws = tuple((s, v) for (s, v) in waits if v > 0)
        steps[eng].append((ws, fn, inc, amt))
        if inc is not None:
            semv[inc] += amt
        return semv[inc] if inc is not None else 0

    bank_free = [None] * 8

    def bank_waits(banks):
        ws = []
        for k in banks:
            if bank_free[k] is not None and bank_free[k] not in ws:
                ws.append(bank_free[k])
        return ws

    cl = [(dvs, dv_d), (idb, idb_d), (gsc, gsc_d), (usc, usc_d), (dsc, dsc_d),
          (biasr, bias_d), (w0r, w0_d), (w0rev, w0rev_d), (idf, idf_d), (mt, mt_d), (eall, eall_d)]
    cl_done = {}
    for dst, src in cl:
        dW_consts = emit(
            "sync",
            (lambda dst, src: lambda e: e.dma_start(out=dst[:], in_=src.ap()))(dst, src),
            inc="dW", amt=16)
        cl_done[dst.name] = dW_consts

    v = emit("vector", lambda e: e.tensor_scalar(
        out=dvc[:], in0=dvs[:], scalar1=0.9, scalar2=None, op0=ALU.max),
        waits=[("dW", cl_done["dvs"])], inc="sV")
    v = emit("vector", lambda e: e.tensor_scalar(
        out=dvc[:], in0=dvc[:], scalar1=1.0, scalar2=None, op0=ALU.min),
        waits=[("sV", v)], inc="sV")
    sV_clip = v

    emit("gpsimd", lambda e: e.dma_start(
        out=w1b[:], in_=w1_d.ap().broadcast_to((128, D))), inc="sO", amt=16)

    NIT = 2 * NI
    wgld = [None] * NIT
    wuld = [None] * NIT
    deqg = [None] * NIT
    dequ = [None] * NIT
    mmg = [None] * NIT
    mmu = [None] * NIT
    evA = [None] * NIT
    evB = [None] * NIT
    hmul = [None] * NIT
    NDIT = 4 * NI
    wdld = [None] * NDIT
    deqd = [None] * NDIT
    mmd = [None] * NDIT
    mmd_bank = {}
    evD = {}

    def p1_dma(it):
        i = it % NI
        ws = [deqg[it - 2]] if it >= 2 else []
        wgld[it] = emit("sync", (lambda it, i: lambda e: e.dma_start(
            out=wgu[:, it % 2, :], in_=wgT3[:, :, i * 128:(i + 1) * 128]))(it, i),
            waits=ws, inc="dW", amt=16)
        ws = [dequ[it - 2]] if it >= 2 else []
        wuld[it] = emit("sync", (lambda it, i: lambda e: e.dma_start(
            out=wuu[:, it % 2, :], in_=wuT3[:, :, i * 128:(i + 1) * 128]))(it, i),
            waits=ws, inc="dW", amt=16)

    def p1_deqg(it, eng="scalar"):
        ws = [("dW", wgld[it])]
        if it >= 2:
            ws.append(("sT", mmg[it - 2]))
        if eng == "gpsimd":
            val = emit(eng, (lambda it: lambda e: e.tensor_scalar(
                out=wgb[:, it % 2, :], in0=wgu[:, it % 2, :], scalar1=-128.0,
                scalar2=None, op0=ALU.add))(it), waits=ws, inc="sO")
            deqg[it] = ("sO", val)
            return
        val = emit(eng, (lambda it: lambda e: e.activation(
            out=wgb[:, it % 2, :], in_=wgu[:, it % 2, :], func=AF.Copy,
            bias=-128.0))(it), waits=ws, inc="sS")
        deqg[it] = ("sS", val)

    def p1_dequ(it, eng="vector"):
        ws = [("dW", wuld[it])]
        if it >= 2:
            ws.append(("sT", mmu[it - 2]))
        if eng == "gpsimd":
            val = emit(eng, (lambda it: lambda e: e.tensor_scalar(
                out=wub[:, it % 2, :], in0=wuu[:, it % 2, :], scalar1=-128.0,
                scalar2=None, op0=ALU.add))(it), waits=ws, inc="sO")
            dequ[it] = ("sO", val)
            return
        if eng == "vector":
            val = emit(eng, (lambda it: lambda e: e.tensor_scalar(
                out=wub[:, it % 2, :], in0=wuu[:, it % 2, :], scalar1=-128.0,
                scalar2=None, op0=ALU.add))(it), waits=ws, inc="sV")
            dequ[it] = ("sV", val)
        else:
            val = emit(eng, (lambda it: lambda e: e.activation(
                out=wub[:, it % 2, :], in_=wuu[:, it % 2, :], func=AF.Copy,
                bias=-128.0))(it), waits=ws, inc="sS")
            dequ[it] = ("sS", val)

    def p1_mm(it, half):
        p = it % 2
        for which in range(2):
            bankk = p if which == 0 else 2 + p
            wb = wgb if which == 0 else wub
            dq = deqg[it] if which == 0 else dequ[it]
            for hb in range(HB):
                ws = []
                if hb == 0:
                    ws = [dq] + bank_waits([bankk])
                    if it == 0 and which == 0:
                        ws.append(("sS", xnT_ready))
                    if it == NI and which == 0:
                        ws.append(("sS", evac_done[NT - 1][1]))
                tt = emit("tensor", (lambda p, wb, hb, bankk, half: lambda e: e.matmul(
                    out=psb(bankk),
                    lhsT=wb[:, p, hb * 128:(hb + 1) * 128],
                    rhs=xnT[:, hb * TOK + half * TH: hb * TOK + half * TH + TH],
                    start=(hb == 0), stop=(hb == HB - 1)))(p, wb, hb, bankk, half),
                    waits=ws, inc="sT" if hb == HB - 1 else None)
            if which == 0:
                mmg[it] = tt
            else:
                mmu[it] = tt

    def p1_evac(it):
        p = it % 2
        ab = it % 2
        i = it % NI
        ws = [("sT", mmg[it])]
        if it >= 2:
            ws.append(("sV", hmul[it - 2]))
        evA[it] = emit("scalar", (lambda p, ab, i: lambda e: e.activation(
            out=at_t[:, ab, :], in_=psb(p), func=AF.Silu,
            scale=gsc[:, i:i + 1]))(p, ab, i), waits=ws, inc="sS")
        bank_free[p] = ("sS", evA[it])
        evB[it] = emit("vector", (lambda p, ab: lambda e: e.tensor_copy(
            out=bt_t[:, ab, :], in_=psb(2 + p)))(p, ab),
            waits=[("sT", mmu[it])], inc="sV")
        bank_free[2 + p] = ("sV", evB[it])
        ws = [("sS", evA[it]), ("sV", evB[it])]
        if it >= NI:
            ws.append(("sT", mmd[NI + i]))
        else:
            if i < 32:
                s = i // 8
                ws += [("sV", norm_done[s + 4]), ("sS", sq_done[s + 4])]
            elif i < 48:
                b2 = (i - 32) // 4
                ws.append(("sT", tp0_done[b2 + 4][1]))
            elif i < 56:
                ws.append(("sS", sq_done[7]))
        hmul[it] = emit("vector", (lambda ab, i: lambda e: e.tensor_tensor(
            out=hp(i), in0=at_t[:, ab, :], in1=bt_t[:, ab, :],
            op=ALU.mult))(ab, i), waits=ws, inc="sV")

    def p2_dma(dit):
        w = (dit // NI) % 2
        i = dit % NI
        ws = [("sV", deqd[dit - 4])] if dit >= 4 else []
        wdld[dit] = emit("sync", (lambda dit, i, w: lambda e: e.dma_start(
            out=wdu[:, dit % 4, :], in_=wdT3[:, i, w * 1024:(w + 1) * 1024]))(dit, i, w),
            waits=ws, inc="dW", amt=16)

    def p2_deq(dit):
        i = dit % NI
        ws = [("dW", wdld[dit])]
        if dit >= 2:
            ws.append(("sT", mmd[dit - 2]))
        deqd[dit] = emit("vector", (lambda dit, i: lambda e: e.tensor_scalar(
            out=wdb[:, dit % 2, :], in0=wdu[:, dit % 4, :], scalar1=-128.0,
            scalar2=usc[:, i:i + 1], op0=ALU.add, op1=ALU.mult))(dit, i),
            waits=ws, inc="sV")

    def p2_mm(dit, half, i):
        q = dit % 2
        for k in range(8):
            ws = []
            if k == 0:
                ws = [("sV", deqd[dit]), ("sV", hmul[half * NI + i])]
            if i == 0:
                ws += bank_waits([k])
            last = (i == NI - 1)
            tt = emit("tensor", (lambda q, k, i: lambda e: e.matmul(
                out=psb(k), lhsT=wdb[:, q, k * 128:(k + 1) * 128],
                rhs=hp(i), start=(i == 0), stop=(i == NI - 1)))(q, k, i),
                waits=ws, inc="sT" if (last or k == 7) else None)
            if last:
                mmd_bank[(dit // NI, k)] = tt
        mmd[dit] = tt

    exp_done = [None] * HB
    wseg_done = [None] * HB
    wd_done = [None] * HB
    attn_done = [[None, None] for _ in range(HB)]
    pmul_done = [None] * HB
    tp_done = [None] * HB
    add_done = [None] * HB
    xsl_ld = [None] * HB
    out_st = [None] * HB

    def h_freed(blocks):
        return ("sT", mmd[3 * NI + max(blocks)])

    def p3_exp(b):
        sbi = b % 7
        s1i = b % 2
        n = (b + 1) * 128
        ws = [("sO", dW_lnrep)]
        if b < 7:
            ws.append(h_freed(range(8 + 4 * sbi, 12 + 4 * sbi)))
        else:
            ws.append(("sT", attn_done[b - 7][1]))
        emit("scalar", (lambda sbi, n: lambda e: e.activation(
            out=strip0(sbi)[:, 0:n], in_=eall[:, 0:n], func=AF.Exp,
            scale=lnrep[:, 0:1]))(sbi, n), waits=ws, inc="sS")
        ws = []
        if b < 2:
            ws.append(h_freed(range(4 * s1i, 4 + 4 * s1i)))
        else:
            ws.append(("sV", wseg_done[b - 2]))
        exp_done[b] = emit("scalar", (lambda s1i, n: lambda e: e.activation(
            out=strip1(s1i)[:, 0:n], in_=eall[:, 0:n], func=AF.Exp,
            scale=lnrep[:, 1:2]))(s1i, n), waits=ws, inc="sS")

    def p3_wbuild(b):
        sbi = b % 7
        nseg = b + 1
        s0_3 = strip0(sbi)[:, 0:nseg * 128].rearrange("p (s f) -> p s f", f=128)
        s1_3 = strip1(b % 2)[:, 0:nseg * 128].rearrange("p (s f) -> p s f", f=128)
        w0bc = w0rev.ap()[:, HB - 1 - b:HB].unsqueeze(2).broadcast_to((128, nseg, 128))
        w1bc = w1b.ap()[:, b * 128:(b + 1) * 128].unsqueeze(1).broadcast_to((128, nseg, 128))
        v1 = emit("vector", (lambda s0_3, w0bc: lambda e: e.tensor_tensor(
            out=s0_3, in0=s0_3, in1=w0bc, op=ALU.mult))(s0_3, w0bc),
            waits=[("sS", exp_done[b]), ("sO", 16)], inc="sV")
        v2 = emit("vector", (lambda s1_3, w1bc: lambda e: e.tensor_tensor(
            out=s1_3, in0=s1_3, in1=w1bc, op=ALU.mult))(s1_3, w1bc),
            inc="sV")
        wseg_done[b] = emit("vector", (lambda s0_3, s1_3: lambda e: e.tensor_tensor(
            out=s0_3, in0=s0_3, in1=s1_3, op=ALU.add))(s0_3, s1_3),
            waits=[("sV", v2)], inc="sV")
        wd_done[b] = emit("vector", (lambda sbi: lambda e: e.tensor_tensor(
            out=strip0(sbi)[:, 0:128], in0=strip0(sbi)[:, 0:128], in1=mt[:],
            op=ALU.mult))(sbi), waits=[("sV", wseg_done[b])], inc="sV")

    def p2_evac(half, w, wit):
        for k in range(8):
            hh = w * 8 + k
            ev = emit("scalar", (lambda k, hh, half: lambda e: e.activation(
                out=gout[:, hh * TOK + half * TH: hh * TOK + half * TH + TH],
                in_=psb(k), func=AF.Silu, scale=dsc[:, hh:hh + 1]))(k, hh, half),
                waits=[("sT", mmd_bank[(wit, k)])], inc="sS")
            evD[(half, w, k)] = ev
            bank_free[k] = ("sS", ev)


    norm_done = [None] * NT
    red_done = [None] * NT
    sq_done = [None] * NT
    var_done = [None] * NT
    xld = [None] * NT
    tp0_done = [[None, None] for _ in range(NT)]
    evac_done = [[None, None] for _ in range(NT)]
    xnTr = xnT.ap().rearrange("p (hb tok) -> p hb tok", tok=TOK)

    def p0_dma(t):
        ws = []
        if t >= 4:
            ws = [("sV", norm_done[t - 4])]
        xld[t] = emit("scalar", (lambda t: lambda e: e.dma_start(
            out=xbuf(t), in_=x3[:, t, :]))(t), waits=ws, inc="dX", amt=16)

    def p0_sq(t):
        b = t % 2
        ws = [("dX", xld[t])]
        if t >= 2:
            ws.append(("sV", var_done[t - 2]))
        sq_done[t] = emit("scalar", (lambda t, b: lambda e: e.activation(
            out=scr, in_=xbuf(t), func=AF.Square,
            accum_out=stats[:, b, 0:1]))(t, b), waits=ws, inc="sS")

    def p0_red(t):
        b = t % 2
        ws = [("dX", xld[t])]
        if t >= 2:
            ws.append(("sV", var_done[t - 2]))
        red_done[t] = emit("vector", (lambda t, b: lambda e: e.reduce_sum(
            out=stats[:, b, 1:2], in_=xbuf(t), axis=AX.X))(t, b),
            waits=ws, inc="sV")

    def p0_chain(t):
        b = t % 2
        ws = [("sV", red_done[t])]
        if t >= 2:
            ws.append(("sS", sqv_arr[t - 2]))
        v = emit("vector", (lambda b: lambda e: e.tensor_scalar(
            out=stats[:, b, 2:3], in0=stats[:, b, 1:2], scalar1=-1.0 / D,
            scalar2=None, op0=ALU.mult))(b),
            waits=ws, inc="sV")
        v = emit("vector", (lambda b: lambda e: e.tensor_tensor(
            out=stats[:, b, 3:4], in0=stats[:, b, 2:3], in1=stats[:, b, 2:3],
            op=ALU.mult))(b), waits=[("sV", v)], inc="sV")
        var_done[t] = emit("vector", (lambda b: lambda e: e.scalar_tensor_tensor(
            out=stats[:, b, 4:5], in0=stats[:, b, 0:1], scalar=1.0 / D,
            in1=stats[:, b, 3:4], op0=ALU.mult, op1=ALU.subtract))(b),
            waits=[("sV", v), ("sS", sq_done[t])], inc="sV")
        veps = emit("vector", (lambda b: lambda e: e.tensor_scalar(
            out=stats[:, b, 3:4], in0=stats[:, b, 4:5], scalar1=1e-5,
            scalar2=None, op0=ALU.add))(b), waits=[("sV", var_done[t])], inc="sV")
        return veps

    def p0_sqrt(t, veps):
        b = t % 2
        return emit("scalar", (lambda b: lambda e: e.activation(
            out=stats[:, b, 5:6], in_=stats[:, b, 3:4], func=AF.Sqrt))(b),
            waits=[("sV", veps)], inc="sS")

    def p0_norm(t, sqv):
        b = t % 2
        rcp = emit("vector", (lambda b: lambda e: e.reciprocal(
            out=stats[:, b, 6:7], in_=stats[:, b, 5:6]))(b),
            waits=[("sS", sqv)], inc="sV")
        ws = [("sV", rcp)]
        if t >= 4:
            ws.append(("sT", tp0_done[t - 4][1]))
        norm_done[t] = emit("vector", (lambda t, b: lambda e: e.tensor_scalar(
            out=xnt_t(t), in0=xbuf(t), scalar1=stats[:, b, 2:3],
            scalar2=stats[:, b, 6:7], op0=ALU.add, op1=ALU.mult))(t, b),
            waits=ws, inc="sV")

    def p0_transp(t):
        for k2 in range(2):
            bank = (2 * t + k2) % 4 if t < 4 else 4 + (2 * t + k2) % 4
            for j in range(8):
                hb = k2 * 8 + j
                ws = []
                if j == 0:
                    ws = [("sV", norm_done[t]), ("dW", cl_done["idb"])]
                    ws += bank_waits([bank])
                tt = emit("tensor", (lambda t, bank, j, hb: lambda e: e.transpose(
                    out=psb_bf(bank)[:, j * 128:(j + 1) * 128],
                    in_=xnt_t(t)[:, hb * 128:(hb + 1) * 128],
                    identity=idb[:]))(t, bank, j, hb),
                    waits=ws, inc="sT" if j == 7 else None)
                if j == 7:
                    tp0_done[t][k2] = tt

    def p0_evac(t):
        for k2 in range(2):
            bank = (2 * t + k2) % 4 if t < 4 else 4 + (2 * t + k2) % 4
            ev = emit("scalar", (lambda bank, k2, t: lambda e: e.activation(
                out=xnTr[:, k2 * 8:(k2 + 1) * 8, t * 128:(t + 1) * 128],
                in_=psb_bf(bank).rearrange("p (j f) -> p j f", f=128),
                func=AF.Copy))(bank, k2, t),
                waits=[("sT", tp0_done[t][k2])], inc="sS")
            evac_done[t][k2] = ev
            bank_free[bank] = ("sS", ev)

    sqv_arr = [None] * NT
    p0_dma(0)
    p0_dma(1)
    emit("scalar", lambda e: e.preload_act_table(AF.Sqrt))
    p0_sq(0)
    p0_red(0)
    p0_sq(1)
    p0_red(1)
    sqv_arr[0] = p0_sqrt(0, p0_chain(0))
    for t in range(NT):
        if t + 2 < NT:
            p0_dma(t + 2)
        if t == 4:
            p0_evac(3)
        if t + 1 < NT:
            sqv_arr[t + 1] = p0_sqrt(t + 1, p0_chain(t + 1))
        if 1 <= t <= 3:
            p0_evac(t - 1)
        p0_norm(t, sqv_arr[t])
        if t == 1:
            p1_dma(0)
            p1_dma(1)
        if t == 2:
            p1_deqg(0)
            p1_deqg(1)
        if t == 3:
            p1_dequ(0)
            p1_dequ(1)
        if t + 2 < NT:
            p0_sq(t + 2)
            p0_red(t + 2)
        if t <= 3:
            p0_transp(t)
    emit("scalar", lambda e: e.preload_act_table(AF.Ln))
    sS_ln = emit("scalar", lambda e: e.activation(
        out=lndv[:], in_=dvc[:], func=AF.Ln), waits=[("sV", sV_clip)], inc="sS")
    dg = emit("gpsimd", lambda e: e.dma_start(out=lnscr_d.ap(), in_=lndv[:]),
              waits=[("sS", sS_ln)], inc="sO", amt=16)
    dW_lnrep = emit("gpsimd", lambda e: e.dma_start(
        out=lnrep[:], in_=lnscr_d.ap().broadcast_to((128, 2))),
        waits=[("sO", dg)], inc="sO", amt=16)
    emit("scalar", lambda e: e.preload_act_table(AF.Silu))

    xnT_ready = evac_done[3][1]

    for half in range(2):
        base = half * NI
        for i in range(NI):
            it = base + i
            if i + 2 < NI:
                p1_dma(it + 2)
            nxt = it + 1
            if i + 1 < NI and not (nxt <= base + 1):
                p1_deqg(nxt)
                p1_dequ(nxt)
            p1_mm(it, half)
            if i == NI - 3:
                for pf in range(4):
                    p2_dma(half * 2 * NI + pf)
            if i == NI - 2:
                p2_deq(half * 2 * NI)
                p2_deq(half * 2 * NI + 1)
            if half == 0 and i < 4:
                p0_transp(4 + i)
            if half == 0 and 1 <= i <= 4:
                p0_evac(3 + i)
            p1_evac(it)
        dbase0 = half * 2 * NI
        if half == 0:
            p1_dma(NI)
            p1_dma(NI + 1)
            p1_deqg(NI)
            p1_dequ(NI, eng="scalar")
            p1_deqg(NI + 1)
            p1_dequ(NI + 1, eng="scalar")
        for w in range(2):
            wit = half * 2 + w
            dbase = wit * NI
            for i in range(NI):
                dit = dbase + i
                if dit >= dbase0 + 2:
                    p2_deq(dit)
                if dit + 4 < dbase0 + 2 * NI:
                    p2_dma(dit + 4)
                p2_mm(dit, half, i)
                if half == 1 and w == 1:
                    if i == 12:
                        emit("scalar", lambda e: e.preload_act_table(AF.Exp))
                        p3_exp(0)
                    elif i == 15:
                        p3_exp(1)
                    elif i == 18:
                        p3_wbuild(0)
                    elif i == 21:
                        p3_exp(2)
                    elif i == 24:
                        p3_wbuild(1)
                    elif i == 27:
                        p3_exp(3)
                    elif i == 30:
                        p3_wbuild(2)
                    elif i == 33:
                        p3_exp(4)
                    elif i == 36:
                        p3_wbuild(3)
                    elif i == 39:
                        p3_exp(5)
                    elif i == 42:
                        p3_wbuild(4)
                    elif i == 45:
                        p3_exp(6)
                    elif i == 48:
                        p3_wbuild(5)
                    elif i == 52:
                        p3_wbuild(6)
            if half == 1 and w == 1:
                emit("scalar", lambda e: e.preload_act_table(AF.Silu))
            p2_evac(half, w, wit)
            if half == 1 and w == 1:
                emit("scalar", lambda e: e.preload_act_table(AF.Exp))

    def p3_xsl(b):
        pb = b % 2
        if b < 2:
            ws = [h_freed(range(46 + pb * 4, 50 + pb * 4))]
        else:
            ws = [("sV", add_done[b - 2])]
        xsl_ld[b] = emit("sync", (lambda b, pb: lambda e: e.dma_start(
            out=xslv(pb).rearrange("p (j f) -> p j f", f=128),
            in_=x3[:, :, b * 128:(b + 1) * 128]))(b, pb), waits=ws, inc="dW", amt=16)

    def p3_attn(b):
        sbi = b % 7
        pb = b % 2
        for th in range(2):
            pa = pb * 2 + th
            for a in range(b + 1):
                d = b - a
                ws = []
                if a == 0:
                    ws = [("sV", wd_done[b])] + bank_waits([pa])
                tt = emit("tensor", (lambda sbi, pa, a, d, th: lambda e: e.matmul(
                    out=psb(pa),
                    lhsT=strip0(sbi)[:, d * 128:(d + 1) * 128],
                    rhs=xnT[:, a * TOK + th * TH: a * TOK + th * TH + TH],
                    start=(a == 0), stop=(a == b)))(sbi, pa, a, d, th),
                    waits=ws, inc="sT" if a == b else None)
            attn_done[b][th] = tt

    def p3_pmul(b):
        pb = b % 2
        ws = [("sT", attn_done[b][1]),
              ("sS", evD[(1, b // 8, b % 8)])]
        if b >= 2:
            ws.append(("sT", tp_done[b - 2]))
        pmul_done[b] = emit("vector", (lambda pb, b: lambda e: e.scalar_tensor_tensor(
            out=pTv(pb).rearrange("p (t f) -> p t f", f=512),
            in0=psap[:, pb * 2:pb * 2 + 2, :],
            scalar=biasr[:, b:b + 1],
            in1=gout[:, b * TOK:(b + 1) * TOK].rearrange("p (t f) -> p t f", f=512),
            op0=ALU.add, op1=ALU.mult))(pb, b), waits=ws, inc="sV")
        bank_free[pb * 2] = ("sV", pmul_done[b])
        bank_free[pb * 2 + 1] = ("sV", pmul_done[b])

    def p3_transp(b):
        pb = b % 2
        for j in range(8):
            bank = 4 + 2 * pb + j // 4
            ws = []
            if j == 0:
                ws = [("sV", pmul_done[b])] + bank_waits([4 + 2 * pb, 5 + 2 * pb])
            tt = emit("tensor", (lambda pb, j, bank: lambda e: e.transpose(
                out=psap[:, bank, (j % 4) * 128:(j % 4) * 128 + 128],
                in_=pTv(pb)[:, j * 128:(j + 1) * 128],
                identity=idf[:]))(pb, j, bank),
                waits=ws, inc="sT" if j == 7 else None)
        tp_done[b] = tt

    def p3_add(b):
        pb = b % 2
        ws = [("sT", tp_done[b]), ("dW", xsl_ld[b])]
        if b < 2:
            ws.append(h_freed(range(54 + pb * 4, 58 + pb * 4)))
        if b >= 2:
            ws.append(("sO", out_st[b - 2]))
        add_done[b] = emit("vector", (lambda pb: lambda e: e.tensor_tensor(
            out=obufv(pb).rearrange("p (t f) -> p t f", f=512),
            in0=psap[:, 4 + 2 * pb:6 + 2 * pb, :],
            in1=xslv(pb).rearrange("p (t f) -> p t f", f=512),
            op=ALU.add))(pb), waits=ws, inc="sV")
        bank_free[4 + 2 * pb] = ("sV", add_done[b])
        bank_free[5 + 2 * pb] = ("sV", add_done[b])

    def p3_store(b):
        pb = b % 2
        out_st[b] = emit("gpsimd", (lambda pb, b: lambda e: e.dma_start(
            out=y3[:, :, b * 128:(b + 1) * 128],
            in_=obufv(pb).rearrange("p (j f) -> p j f", f=128)))(pb, b),
            waits=[("sV", add_done[b])], inc="sO", amt=16)

    for b in range(HB):
        p3_attn(b)
        if b + 7 < HB:
            p3_exp(b + 7)
        if b >= 2:
            p3_transp(b - 2)
        p3_pmul(b)
        if b >= 2:
            p3_add(b - 2)
        p3_xsl(b)
        if b + 7 < HB:
            p3_wbuild(b + 7)
        if b >= 2:
            p3_store(b - 2)
    for b in (HB - 2, HB - 1):
        p3_transp(b)
        p3_add(b)
        p3_store(b)
    emit("gpsimd", lambda e: e.nop(), waits=[("sO", semv["sO"])])

    sems = {}
    for name in ("dW", "dX", "sT", "sS", "sV", "sO"):
        sems[name] = ctx.enter_context(nc.semaphore(name))
    block = ctx.enter_context(nc.Block())

    def runner(engname):
        def _run(eng):
            for ws, fn, inc, amt in steps[engname]:
                for (s, vv) in ws:
                    eng.wait_ge(sems[s], vv)
                ins = fn(eng)
                if inc is not None:
                    ins.then_inc(sems[inc], amt)
        return _run

    block.sync(runner("sync"))
    block.scalar(runner("scalar"))
    block.vector(runner("vector"))
    block.tensor(runner("tensor"))
    block.gpsimd(runner("gpsimd"))
    ctx.close()
    return nc


def _low_byte(w):
    w = np.ascontiguousarray(np.asarray(w, dtype=np.int32))
    assert np.little_endian
    return w.view(np.uint8).reshape(w.shape[0], w.shape[1], 4)[:, :, 0]


def kernel(L0_attn_x, L0_attn_weight, L0_attn_bias, L0_attn_decay_value,
           L0_ffn_gate_w, L0_ffn_gate_scale, L0_ffn_up_w, L0_ffn_up_scale,
           L0_ffn_down_w, L0_ffn_down_scale):
    x = np.ascontiguousarray(np.asarray(L0_attn_x, dtype=np.float32).reshape(SEQ, D))
    wgT = np.ascontiguousarray(_low_byte(L0_ffn_gate_w).T)
    wuT = np.ascontiguousarray(_low_byte(L0_ffn_up_w).T)
    wdT = np.ascontiguousarray(_low_byte(L0_ffn_down_w).T)
    gsc = np.ascontiguousarray(np.asarray(L0_ffn_gate_scale, np.float32).reshape(NI, 128).T)
    usc = np.ascontiguousarray(np.asarray(L0_ffn_up_scale, np.float32).reshape(NI, 128).T)
    dsc = np.ascontiguousarray(np.asarray(L0_ffn_down_scale, np.float32).reshape(HB, 128).T)
    biasr = np.ascontiguousarray(np.asarray(L0_attn_bias, np.float32).reshape(HB, 128).T)
    wgt = np.asarray(L0_attn_weight, np.float32)
    w0r = np.ascontiguousarray(wgt[0].reshape(HB, 128).T)
    w0rev = np.ascontiguousarray(w0r[:, ::-1])
    w1row = np.ascontiguousarray(wgt[1].reshape(1, D))
    dvrow = np.ascontiguousarray(np.asarray(L0_attn_decay_value, np.float32).reshape(1, 2))
    eall = (np.arange(D)[None, :] - np.arange(128)[:, None]).astype(np.float16)
    mtile = (np.arange(128)[None, :] >= np.arange(128)[:, None]).astype(ml_dtypes.bfloat16)
    idbf = np.eye(128, dtype=ml_dtypes.bfloat16)
    idf32 = np.eye(128, dtype=np.float32)

    shared = {"wgT": wgT, "wuT": wuT, "wdT": wdT, "gsc": gsc, "usc": usc,
              "dsc": dsc, "biasr": biasr, "w0r": w0r, "w0rev": w0rev, "w1row": w1row,
              "dvrow": dvrow, "eall": eall, "mtile": mtile, "idbf": idbf,
              "idf32": idf32}
    in_maps = []
    for c in range(NCORE):
        m = dict(shared)
        m["x"] = np.ascontiguousarray(x[c * TOK:(c + 1) * TOK])
        in_maps.append(m)

    nc = build_program()
    res = run_bass_kernel_spmd(nc, in_maps, core_ids=list(range(NCORE)))
    y = np.concatenate([res.results[c]["y"] for c in range(NCORE)], axis=0)
    return y.reshape(1, SEQ, D)
